# version 01
# baseline (speedup 1.0000x reference)
import math
import numpy as np
import ml_dtypes
import concourse.bass as bass
import concourse.mybir as mybir

F32 = mybir.dt.float32
F16 = mybir.dt.float16
BF16 = mybir.dt.bfloat16
AF = mybir.ActivationFunctionType
ALU = mybir.AluOpType
AX = mybir.AxisListType

MAGIC = float(np.float32(1.5 * 2 ** 23))
C_SM = float(1.0 / math.sqrt(128.0))
C_FW = float(np.float32(1.0 / np.sqrt(np.float32(8192.0))))
SQRT2 = float(np.float32(np.float32(128.0) * np.float32(C_FW)))
INV128 = float(np.float32(1.0 / 128.0))
INV2048 = float(np.float32(1.0 / 2048.0))
INV2_24 = float(np.float32(2.0 ** -24))
EPS_LN = 1e-5
EPS_Q = 1e-5

S, D, E, HPC, TPC, II = 4096, 2048, 128, 4, 1024, 8192
RG4 = [[0, 1, 2, 3], [4, 5, 6, 7]]
RG8 = [[0, 1, 2, 3, 4, 5, 6, 7]]


def hadamard(n):
    h = np.array([[1.0]])
    while h.shape[0] < n:
        h = np.block([[h, h], [h, -h]])
    return h


class Bank:
    __slots__ = ("ap", "free")

    def __init__(self, ap):
        self.ap = ap
        self.free = []


class KB:
    def __init__(self, nc, stage=6, taps=(), sub=99):
        self.nc = nc
        self.stage = stage
        self.sub = sub
        self.taps = set(taps)
        self.cur = None
        self.e = None
        self._alloc()

    def on(self, *names):
        return self.cur in names

    def tinc(self, sem, inst, k=1):
        if inst is not None:
            inst.then_inc(self.sems[sem], k)
        self.cnt[sem] += k

    def wt(self, who, sem, val=None):
        if self.on(*who):
            self.e.wait_ge(self.sems[sem], self.cnt[sem] if val is None else val)

    def dr(self, *who):
        if self.on(*who):
            self.e.drain()

    def dma(self, queue, sem, out, in_, slow=False):
        if self.on(queue):
            self.e.dma_start(out=out, in_=in_, allow_slow_non_contiguous=slow) \
                .then_inc(self.sems[sem], 16)
        self.cnt[sem] += 16

    def dmat(self, queue, sem, out, in_):
        if self.on(queue):
            self.e.dma_start_transpose(out, in_).then_inc(self.sems[sem], 16)
        self.cnt[sem] += 16

    def bank_acquire(self, b):
        if self.on("tensor"):
            for sem, val in b.free:
                self.e.wait_ge(self.sems[sem], val)

    def bank_release(self, b, sem, add=False):
        if add:
            b.free.append((sem, self.cnt[sem]))
        else:
            b.free = [(sem, self.cnt[sem])]

    def vop(self, fname, *a, **kw):
        self.tinc("sV", getattr(self.e, fname)(*a, **kw) if self.on("vector") else None)

    def sop(self, fname, *a, **kw):
        self.tinc("sS", getattr(self.e, fname)(*a, **kw) if self.on("scalar") else None)

    def sop_d(self, fname, *a, **kw):
        if self.on("scalar"):
            getattr(self.e, fname)(*a, **kw)
            self.tinc("sS", self.e.drain())
        else:
            self.tinc("sS", None)

    def bcast_row(self, row_f32, hi, lo, bank, n, exact=True):
        if "bcast_T" in self.mark:
            self.wt(["vector"], "sT", self.mark["bcast_T"])
        self.vop("tensor_copy", hi, row_f32)
        if exact:
            self.dr("vector")
            self.vop("tensor_tensor", lo, row_f32, hi, ALU.subtract)
        self.wt(["tensor"], "sV")
        self.bank_acquire(bank)
        if exact:
            if self.on("tensor"):
                self.e.matmul(bank.ap[:, 0:n], self.s_onesrow[:, :], hi, start=True, stop=False)
            self.tinc("sT", self.e.matmul(bank.ap[:, 0:n], self.s_onesrow[:, :], lo,
                                          start=False, stop=True) if self.on("tensor") else None)
        else:
            self.tinc("sT", self.e.matmul(bank.ap[:, 0:n], self.s_onesrow[:, :], hi,
                                          start=True, stop=True) if self.on("tensor") else None)
        self.mark["bcast_T"] = self.cnt["sT"]

    def _alloc(self):
        nc = self.nc
        dt = nc.dram_tensor
        self.xb_own = dt("xb_own", [TPC, D], F32, kind="ExternalInput").ap()
        self.wq = dt("wq", [HPC, E, D], F32, kind="ExternalInput").ap()
        self.wk = dt("wk", [HPC, E, D], F32, kind="ExternalInput").ap()
        self.wv = dt("wv", [HPC, E, D], F32, kind="ExternalInput").ap()
        self.wo = dt("wo", [16, D, E], F32, kind="ExternalInput").ap()
        self.wg_sl = dt("wg_sl", [1024, D], F32, kind="ExternalInput").ap()
        self.wu_sl = dt("wu_sl", [1024, D], F32, kind="ExternalInput").ap()
        self.wd_sl = dt("wd_sl", [256, II], F32, kind="ExternalInput").ap()
        self.out = dt("out", [TPC, D], F32, kind="ExternalOutput").ap()

        f16 = np.float16
        self.c_h128 = nc.inline_tensor(hadamard(128).astype(f16), "h128")
        self.c_ident = nc.inline_tensor(np.eye(128, dtype=f16), "ident")
        self.c_ident32 = nc.inline_tensor(np.eye(128, dtype=np.float32), "ident32")
        self.c_onesrow = nc.inline_tensor(np.ones((1, 128), dtype=ml_dtypes.bfloat16), "onesrow")
        self.c_onescol = nc.inline_tensor(np.ones((128, 1), dtype=f16), "onescol")
        self.c_onescol_bf = nc.inline_tensor(np.ones((128, 1), dtype=ml_dtypes.bfloat16), "onescol_bf")
        amask = np.zeros((128, 128), dtype=np.float32)
        for q in range(128):
            amask[q, q + 1:] = -1e9
        self.c_amask = nc.inline_tensor(amask, "amask")
        self.c_magic = nc.inline_tensor(np.tile(np.array([[MAGIC, -MAGIC]], dtype=np.float32), (128, 1)), "magic")

        self.wqkv_scr = dt("wqkv_scr", [3, HPC * E, D], F16, kind="Internal").ap()
        self.wo_scr = dt("wo_scr", [16, D, E], F16, kind="Internal").ap()
        self.hag_in = dt("hag_in", [TPC, D], F16, kind="Internal").ap()
        self.hag_out = dt("hag_out", [S, D], F16, kind="Internal").ap()
        self.zag_in = dt("zag_in", [HPC * E, S], F16, kind="Internal").ap()
        self.zag_out = dt("zag_out", [16 * E, S], F16, kind="Internal").ap()
        self.ar_in = dt("ar_in", [1, 4], F32, kind="Internal").ap()
        self.ar_out = dt("ar_out", [1, 4], F32, kind="Internal", addr_space="Shared").ap()
        self.tg_in = dt("tg_in", [D, 1024], F16, kind="Internal").ap()
        self.tg_out = dt("tg_out", [8 * D, 1024], F16, kind="Internal", addr_space="Shared").ap()
        self.tu_in = dt("tu_in", [D, 1024], F16, kind="Internal").ap()
        self.tu_out = dt("tu_out", [8 * D, 1024], F16, kind="Internal", addr_space="Shared").ap()
        self.td_in = dt("td_in", [II, 256], F16, kind="Internal").ap()
        self.td_out = dt("td_out", [8 * II, 256], F16, kind="Internal", addr_space="Shared").ap()
        self.x1buf = dt("x1buf", [TPC, D], F32, kind="Internal").ap()
        self.ffnbuf = dt("ffnbuf", [TPC, D], F16, kind="Internal").ap()

        self.tap_t = {}
        tap_specs = {
            "h": ([S, D], F16), "qt": ([HPC * E, S], F16), "kt": ([HPC * E, S], F16),
            "v": ([32 * 128, HPC * E], F16), "z": ([HPC * E, S], F16),
            "x1": ([TPC, D], F32), "xq": ([TPC, D], F16), "tg": ([D, 1024], F16),
            "td": ([II, 256], F16), "biasb": ([1, 8], F32), "wsc": ([1, 8], F32),
            "inter": ([64 * 128, 512], F16), "zz": ([64 * 128, 512], F16),
            "xq2": ([64 * 128, 512], F16), "ffn": ([TPC, D], F32),
            "hT": ([128, 16 * 512], F16), "wqt": ([128, 16 * 512], F16),
            "ao": ([TPC, D], F32), "zsl": ([TPC, 16 * 128], F16), "xr": ([TPC, D], F32),
            "ssm": ([128, 64], F32), "arr": ([1, 4], F32),
        }
        for t in self.taps:
            shp, dtp = tap_specs[t]
            self.tap_t[t] = dt("tap_" + t, shp, dtp, kind="ExternalOutput").ap()

        self._off = ((nc.sbuf_base + 31) // 32) * 32

        def sb(name, shape, dtyp, at=None):
            if at is not None:
                self._off = at
            t = nc.alloc_sbuf_tensor_at(name, shape, dtyp, offset=self._off)
            nbytes = int(np.prod(shape[1:])) * np.dtype(mybir.dt.np(dtyp)).itemsize
            self._off += (nbytes + 31) // 32 * 32
            return t.ap()

        self.s_h128 = sb("s_h128", [128, 128], F16)
        self.s_ident = sb("s_ident", [128, 128], F16)
        self.s_ident32 = sb("s_ident32", [128, 128], F32)
        self.s_onesrow = sb("s_onesrow", [1, 128], BF16)
        self.s_onescol = sb("s_onescol", [128, 1], F16)
        self.s_onescol_bf = sb("s_onescol_bf", [128, 1], BF16)
        self.s_amask = sb("s_amask", [128, 128], F32)
        self.s_magic = sb("s_magic", [128, 2], F32)
        self.s_small = sb("s_small", [128, 64], F32)
        self.s_row = sb("s_row", [1, 532], F32)
        self.s_rhi = sb("s_rhi", [1, 532], BF16)
        self.s_rlo = sb("s_rlo", [1, 532], BF16)
        self.s_r4 = sb("s_r4", [128, 256], F32)
        self.s_biasB = sb("s_biasB", [128, 8], F32)
        self.s_wsc = sb("s_wsc", [128, 8], F32)
        self.s_lnst = sb("s_lnst", [128, 48], F32)
        self.s_bhl = sb("s_bhl", [128, 8], BF16)
        self.w_chunk = sb("w_chunk", [128, 2 * 1024], F32)
        self.w_zq = sb("w_zq", [128, 1024], F32)
        self.w_r = sb("w_r", [128, 1024], F32)
        self.w_t = sb("w_t", [128, 1024], F16)
        self.w_stage = sb("w_stage", [128, 1024], F16)
        pe = self._off
        self.A_BASE = AB = pe
        LIMIT = nc.sbuf_top
        K = 1024

        def at(name, shape, dtyp, off):
            t = nc.alloc_sbuf_tensor_at(name, shape, dtyp, offset=AB + off)
            nbytes = int(np.prod(shape[1:])) * np.dtype(mybir.dt.np(dtyp)).itemsize
            assert AB + off + nbytes <= LIMIT, f"{name} overflows: {AB + off + nbytes}"
            return t.ap()

        self.l_x8 = at("l_x8", [128, 8 * D], F32, 0)
        self.l_h = at("l_h", [128, 8 * D], F16, 64 * K)

        self.a_qt = at("a_qt", [128, HPC * S], F16, 0)
        self.a_kt = at("a_kt", [128, HPC * S], F16, 32 * K)
        self.a_v = at("a_v", [128, 32 * 512], F16, 64 * K)
        self.a_hT = at("a_hT", [128, 2 * 16 * 512], F16, 96 * K)
        self.a_wqt = at("a_wqt", [128, 16 * 512], F16, 128 * K)
        self.a_wkt = at("a_wkt", [128, 16 * 512], F16, 144 * K)
        self.a_wvt = at("a_wvt", [128, 16 * 512], F16, 160 * K)
        self.a_sS = at("a_sS", [128, 4096], F32, 128 * K)
        self.a_P = at("a_P", [128, 4096], F16, 144 * K)
        self.a_Pn = at("a_Pn", [128, 4096], F16, 152 * K)
        self.a_ptT = at("a_ptT", [128, 32 * 128], F16, 160 * K)
        self.a_zb = at("a_zb", [128, 2 * 512], F16, 168 * K)
        self.a_mx = at("a_mx", [128, 16], F32, 170 * K)

        self.c_wot = at("c_wot", [128, 16 * D], F16, 0)
        self.c_xqT = at("c_xqT", [128, 16 * TPC], F16, 64 * K)
        self.c_zsl = at("c_zsl", [128, 2 * 16 * 128], F16, 96 * K)
        self.c_x1 = at("c_x1", [128, D], F32, 104 * K)
        self.c_xq = at("c_xq", [128, D], F16, 112 * K)
        self.c_ao = at("c_ao", [128, D], F32, 128 * K)
        self.c_h2 = at("c_h2", [128, D], F32, 136 * K)
        self.c_u = at("c_u", [128, D], F32, 144 * K)
        self.c_xr = at("c_xr", [128, D], F32, 152 * K)

        self.f_z = at("f_z", [128, 64 * 512], F16, 0)
        self.f_tg = at("f_tg", [128, 2 * 16 * 256], F16, 96 * K)
        self.f_tu = at("f_tu", [128, 2 * 16 * 256], F16, 112 * K)
        self.f_dstream = at("f_dstream", [128, 2 * 8192], F16, 96 * K)
        self.f_sg = at("f_sg", [128, 2 * 512], F16, 160 * K)
        self.f_inter = at("f_inter", [128, 2 * 512], F16, 162 * K)
        self.f_scs = at("f_scs", [128, 512], F32, 164 * K)
        self.f_invs = at("f_invs", [128, 512], F32, 166 * K)
        self.f_rq = at("f_rq", [128, 2 * 512], F32, 168 * K)
        self.f_qq = at("f_qq", [128, 1 * 512], F32, 172 * K)
        self.f_mtok = at("f_mtok", [128, 16], F32, 174 * K)
        self.f_sp = at("f_sp", [128, 2 * 512], F16, 160 * K)
        self.f_ffT = at("f_ffT", [128, 512], F16, 162 * K)
        self.f_ffT2 = at("f_ffT2", [128, 512], F16, 163 * K)
        WQB = pe - AB - (2 * 1024 * 4 + 1024 * 4 + 1024 * 4 + 1024 * 2 + 1024 * 2)
        self.f_macc = at("f_macc", [128, 6 * 512], F32, WQB)
        self.f_abs = at("f_abs", [128, 2 * 512], F32, WQB + 12 * K)
        self.f_uq = at("f_uq", [128, 2 * 512], F32, WQB + 16 * K)

        self.PB = [Bank(nc.alloc_psum_tensor(f"psb{i}", [128, 512], F32).ap()) for i in range(6)]
        self.PT = Bank(nc.alloc_psum_tensor("ps_t", [128, 1024], F16).ap())
        self.PX = Bank(nc.alloc_psum_tensor("ps_x", [128, 512], F32).ap())

        names = ["sL", "sQ", "sG", "sCC", "sT", "sV", "sS", "sP", "sO",
                 "dCst", "dX8", "dHst", "dWT", "dHT0", "dHT1", "dWq0", "dWq1",
                 "dWst", "dZst0", "dZst1", "dWot", "dZsl0", "dZsl1", "dXr",
                 "dX1", "dTg0", "dTg1", "dDs0", "dDs1", "dFst", "dF2", "dFr",
                 "dAr"]
        self.sems = {}
        self._sem_ctx = []
        for n in names:
            cm = nc.semaphore(n)
            self.sems[n] = cm.__enter__()
            self._sem_ctx.append(cm)
        self.cnt = {n: 0 for n in names}
        self.mark = {}

    def build(self):
        with self.nc.Block() as block:
            block.sync(lambda e: self.emit("sync", e))
            block.tensor(lambda e: self.emit("tensor", e))
            block.vector(lambda e: self.emit("vector", e))
            block.scalar(lambda e: self.emit("scalar", e))
            block.gpsimd(lambda e: self.emit("gpsimd", e))
        return self.nc

    def emit(self, name, eng):
        self.cur = name
        self.e = eng
        self.cnt = {n: 0 for n in self.sems}
        self.mark = {}
        self._qreg = None
        for b in self.PB + [self.PT, self.PX]:
            b.free = []
        self.program()

    def program(self):
        self.ph0_consts()
        if self.stage >= 1 and self.sub not in (20,):
            self.ph1_ln1()
        if self.stage >= 5:
            self.ph5_wquant()
        if self.stage >= 2 and self.sub != 20:
            self.ph2_proj()
        if self.stage >= 3:
            self.ph3_attn()
        if self.stage >= 4:
            self.ph4_post()
        if self.stage >= 6:
            self.ph6_ffn()
        self.fin()

    def ph0_consts(self):
        self.dma("sync", "dCst", self.s_h128[:], self.c_h128.ap())
        self.dma("sync", "dCst", self.s_ident[:], self.c_ident.ap())
        self.dma("sync", "dCst", self.s_ident32[:], self.c_ident32.ap())
        self.dma("sync", "dCst", self.s_onesrow[:], self.c_onesrow.ap())
        self.dma("sync", "dCst", self.s_onescol[:], self.c_onescol.ap())
        self.dma("sync", "dCst", self.s_onescol_bf[:], self.c_onescol_bf.ap())
        self.dma("sync", "dCst", self.s_amask[:], self.c_amask.ap())
        self.dma("sync", "dCst", self.s_magic[:, :], self.c_magic.ap())
        self.mark["consts_L"] = self.cnt["dCst"]
        self.dma("gpsimd", "sG", self.wqkv_scr[0].rearrange("(h e) d -> h e d", h=HPC), self.wq)
        self.dma("gpsimd", "sG", self.wqkv_scr[1].rearrange("(h e) d -> h e d", h=HPC), self.wk)
        self.dma("gpsimd", "sG", self.wqkv_scr[2].rearrange("(h e) d -> h e d", h=HPC), self.wv)
        self.dma("gpsimd", "sG", self.wo_scr, self.wo)
        self.mark["wcast_G"] = self.cnt["sG"]

    def ph1_ln1(self):
        st = self.s_lnst
        for t in range(8):
            self.dma("sync", "dX8", self.l_x8[:, t * D:(t + 1) * D],
                     self.xb_own[t * 128:(t + 1) * 128, :])
        if self.sub in (22, 23):
            if self.sub == 23:
                return
            self.wt(["vector"], "dX8")
            for t in range(8):
                self.vop("tensor_copy", self.l_h[:, t * D:(t + 1) * D],
                         self.l_x8[:, t * D:(t + 1) * D])
            self.wt(["sync"], "sV")
            for t in range(8):
                self.dma("sync", "dHst", self.hag_in[t * 128:(t + 1) * 128, :],
                         self.l_h[:, t * D:(t + 1) * D])
            self.wt(["gpsimd"], "dHst")
            self.tinc("sCC", self.e.collective_compute(
                "AllGather", ALU.bypass, replica_groups=RG4,
                ins=[self.hag_in], outs=[self.hag_out]) if self.on("gpsimd") else None)
            self.mark["h_ready"] = self.cnt["sCC"]
            return
        self.wt(["vector"], "dX8")
        self.wt(["vector"], "dCst")
        for t in range(8):
            xt = self.l_x8[:, t * D:(t + 1) * D]
            self.vop("reduce_sum", st[:, t:t + 1], xt, AX.X)
            self.vop("scalar_tensor_tensor", self.c_u[:, :], xt, 1.0, xt,
                     ALU.mult, ALU.mult, accum_out=st[:, 8 + t:9 + t])
            self.dr("vector")
        self.dr("vector")
        if self.on("vector"):
            self.e.tensor_scalar(st[:, 16:24], st[:, 0:8], INV2048, None, ALU.mult)
            self.e.tensor_scalar(st[:, 24:32], st[:, 8:16], INV2048, None, ALU.mult)
            self.e.drain()
            self.e.tensor_tensor(st[:, 40:48], st[:, 16:24], st[:, 16:24], ALU.mult)
            self.e.drain()
            self.e.scalar_tensor_tensor(st[:, 32:40], st[:, 24:32], 1.0, st[:, 40:48],
                                        ALU.mult, ALU.subtract)
            self.e.drain()
            self.e.tensor_scalar(self.s_small[:, 0:8], st[:, 32:40], EPS_LN, None, ALU.add)
            self.e.drain()
            self.e.reciprocal(self.s_small[:, 8:16], self.s_small[:, 0:8])
            self.e.drain()
        self.vop("nop")
        self.wt(["scalar"], "sV")
        self.sop_d("activation", st[:, 40:48], self.s_small[:, 8:16], AF.Sqrt)
        self.wt(["vector"], "sS")
        for t in range(8):
            self.vop("tensor_scalar", self.l_h[:, t * D:(t + 1) * D],
                     self.l_x8[:, t * D:(t + 1) * D],
                     st[:, 16 + t:17 + t], st[:, 40 + t:41 + t], ALU.subtract, ALU.mult)
        self.wt(["sync"], "sV")
        for t in range(8):
            self.dma("sync", "sL", self.hag_in[t * 128:(t + 1) * 128, :],
                     self.l_h[:, t * D:(t + 1) * D])
        self.wt(["gpsimd"], "sL")
        self.tinc("sCC", self.e.collective_compute(
            "AllGather", ALU.bypass, replica_groups=RG4,
            ins=[self.hag_in], outs=[self.hag_out]) if self.on("gpsimd") else None)
        self.mark["h_ready"] = self.cnt["sCC"]
        if "h" in self.tap_t:
            self.wt(["sync"], "sCC")
            self.dma("sync", "sO", self.tap_t["h"], self.hag_out)

    def _wq_chunks(self):
        chunks = []
        for i in range(8):
            for j in range(2):
                chunks.append(("g", self.wg_sl[i * 128:(i + 1) * 128, j * 1024:(j + 1) * 1024],
                               self.tg_in[j * 1024:(j + 1) * 1024, i * 128:(i + 1) * 128]))
        for i in range(8):
            for j in range(2):
                chunks.append(("u", self.wu_sl[i * 128:(i + 1) * 128, j * 1024:(j + 1) * 1024],
                               self.tu_in[j * 1024:(j + 1) * 1024, i * 128:(i + 1) * 128]))
        for r in range(2):
            for j in range(8):
                chunks.append(("d", self.wd_sl[r * 128:(r + 1) * 128, j * 1024:(j + 1) * 1024],
                               self.td_in[j * 1024:(j + 1) * 1024, r * 128:(r + 1) * 128]))
        return chunks

    def ph5_wquant(self):
        chunks = self._wq_chunks()
        self.vop("memset", self.s_small[:, 16:64], 0.0)
        self.mark["wq_zero"] = self.cnt["sV"]
        self.wt(["scalar"], "sV", self.mark["wq_zero"])
        for idx, (w, src, dst) in enumerate(chunks):
            buf = idx % 2
            wc = self.w_chunk[:, buf * 1024:(buf + 1) * 1024]
            dump = self.w_zq if idx % 2 == 0 else self.w_r
            if idx >= 2:
                self.wt(["sync"], "sS", self.mark[f"wqa_{idx - 2}"])
            self.dma("sync", f"dWq{buf}", wc, src)
            self.wt(["scalar"], f"dWq{buf}")
            if idx >= 2:
                self.wt(["scalar"], "sV", self.mark[f"wqared_{idx - 2}"])
            self.sop_d("activation", dump[:, :], wc, AF.Abs)
            self.mark[f"wqa_{idx}"] = self.cnt["sS"]
            self.wt(["vector"], "sS", self.mark[f"wqa_{idx}"])
            self.vop("tensor_reduce", self.s_small[:, 16 + idx:17 + idx], dump[:, :],
                     AX.X, ALU.add)
            self.mark[f"wqared_{idx}"] = self.cnt["sV"]
        if "ssm" in self.tap_t:
            self.wt(["sync"], "sV")
            self.dma("sync", "sO", self.tap_t["ssm"], self.s_small[:, :])
        self.wt(["vector"], "sS")
        if self.on("vector"):
            self.e.memset(self.s_small[:, 11:12], 0.0)
            self.e.reduce_sum(self.s_small[:, 8:9], self.s_small[:, 16:32], AX.X)
            self.e.reduce_sum(self.s_small[:, 9:10], self.s_small[:, 32:48], AX.X)
            self.e.reduce_sum(self.s_small[:, 10:11], self.s_small[:, 48:64], AX.X)
            self.e.drain()
            self.e.tensor_copy(self.s_bhl[:, 0:4], self.s_small[:, 8:12])
            self.e.drain()
            self.e.tensor_tensor(self.s_bhl[:, 4:8], self.s_small[:, 8:12],
                                 self.s_bhl[:, 0:4], ALU.subtract)
            self.e.drain()
        self.vop("nop")
        self.wt(["tensor"], "sV")
        self.bank_acquire(self.PX)
        if self.on("tensor"):
            self.e.matmul(self.PX.ap[0:1, 0:4], self.s_onescol_bf[:, :], self.s_bhl[:, 0:4],
                          start=True, stop=False)
        self.tinc("sT", self.e.matmul(self.PX.ap[0:1, 0:4], self.s_onescol_bf[:, :],
                                      self.s_bhl[:, 4:8], start=False, stop=True)
                  if self.on("tensor") else None)
        self.wt(["vector"], "sT")
        self.vop("tensor_copy", self.s_row[0:1, 0:4], self.PX.ap[0:1, 0:4])
        self.bank_release(self.PX, "sV")
        self.wt(["sync"], "sV")
        self.dma("sync", "dAr", self.ar_in[:], self.s_row[0:1, 0:4])
        self.wt(["gpsimd"], "dAr")
        self.tinc("sCC", self.e.collective_compute(
            "AllReduce", ALU.add, replica_groups=RG8,
            ins=[self.ar_in], outs=[self.ar_out]) if self.on("gpsimd") else None)
        self.wt(["sync"], "sCC")
        if "arr" in self.tap_t:
            self.dma("sync", "sO", self.tap_t["arr"], self.ar_out[:])
        self.dma("sync", "dAr", self.s_row[0:1, 8:12], self.ar_out[:])
        self.wt(["vector"], "dAr")
        self.bcast_row(self.s_row[0:1, 8:12], self.s_rhi[0:1, 0:4], self.s_rlo[0:1, 0:4],
                       self.PX, 4, exact=True)
        self.wt(["vector"], "sT")
        if self.on("vector"):
            self.e.tensor_scalar(self.s_small[:, 44:48], self.PX.ap[0:128, 0:4],
                                 INV2_24, None, ALU.mult)
            self.e.drain()
            self.e.tensor_scalar(self.s_wsc[:, 0:4], self.s_small[:, 44:48], EPS_Q, None, ALU.max)
            self.e.drain()
            self.e.reciprocal(self.s_wsc[:, 4:8], self.s_wsc[:, 0:4])
            self.e.drain()
        self.vop("nop")
        self.bank_release(self.PX, "sV")
        self.mark["wsc_ready"] = self.cnt["sV"]
        if "wsc" in self.tap_t:
            self.wt(["sync"], "sV")
            self.dma("sync", "sO", self.tap_t["wsc"], self.s_wsc[0:1, :])

        widx_of = {"g": 0, "u": 1, "d": 2}
        last_store = 0
        for idx, (w, src, dst) in enumerate(chunks):
            buf = idx % 2
            wc = self.w_chunk[:, buf * 1024:(buf + 1) * 1024]
            if idx >= 2:
                self.wt(["sync"], "sS", self.mark[f"wqq_{idx - 2}"])
            else:
                self.wt(["sync"], "sV", self.mark[f"wqared_{46 + idx}"])
            self.dma("sync", f"dWq{buf}", wc, src)
            self.wt(["scalar"], f"dWq{buf}")
            self.wt(["scalar"], "sV", self.mark["wsc_ready"])
            if idx >= 1:
                self.wt(["scalar"], "sV", self.mark[f"wqr_{idx - 1}"])
            wi = widx_of[w]
            self.sop_d("activation", self.w_zq[:, :], wc, AF.Identity,
                       bias=self.s_magic[:, 0:1], scale=self.s_wsc[:, 4 + wi:5 + wi])
            self.mark[f"wqq_{idx}"] = self.cnt["sS"]
            self.wt(["vector"], "sS")
            self.vop("tensor_scalar", self.w_r[:, :], self.w_zq[:, :], MAGIC, -1.0,
                     ALU.subtract, ALU.max)
            self.mark[f"wqr_{idx}"] = self.cnt["sV"]
            self.dr("vector")
            if idx >= 1:
                self.wt(["vector"], "sT", self.mark[f"wqt_{idx - 1}"])
            self.vop("tensor_scalar", self.w_t[:, :], self.w_r[:, :], 1.0, None, ALU.min)
            self.mark[f"wqm_{idx}"] = self.cnt["sV"]
            self.wt(["tensor"], "sV", self.mark[f"wqm_{idx}"])
            self.bank_acquire(self.PT)
            for c8 in range(8):
                self.tinc("sT", self.e.transpose(
                    self.PT.ap[:, c8 * 128:(c8 + 1) * 128],
                    self.w_t[:, c8 * 128:(c8 + 1) * 128], self.s_ident[:, :])
                    if self.on("tensor") else None)
            self.mark[f"wqt_{idx}"] = self.cnt["sT"]
            self.wt(["vector"], "sT")
            self.wt(["vector"], "dWst", last_store)
            self.vop("tensor_copy", self.w_stage[:, :], self.PT.ap[:, :])
            self.bank_release(self.PT, "sV")
            self.wt(["sync"], "sV")
            self.dma("sync", "dWst", dst.rearrange("(c p) f -> p c f", p=128),
                     self.w_stage.rearrange("p (c f) -> p c f", c=8))
            last_store = self.cnt["dWst"]
        self.wt(["gpsimd"], "dWst")
        for i_, o_ in ((self.tg_in, self.tg_out), (self.tu_in, self.tu_out),
                       (self.td_in, self.td_out)):
            self.tinc("sCC", self.e.collective_compute(
                "AllGather", ALU.bypass, replica_groups=RG8,
                ins=[i_], outs=[o_]) if self.on("gpsimd") else None)
        self.mark["t_ready"] = self.cnt["sCC"]
        if "tg" in self.tap_t:
            self.wt(["sync"], "sCC")
            self.dma("sync", "sO", self.tap_t["tg"], self.tg_in)
        if "td" in self.tap_t:
            self.wt(["sync"], "sCC")
            self.dma("sync", "sO", self.tap_t["td"], self.td_in)

    def ph2_proj(self):
        self.wt(["sync"], "sG", self.mark["wcast_G"])
        self.wt(["sync"], "sCC", self.mark["h_ready"])
        for p3, dst in ((0, self.a_wqt), (1, self.a_wkt), (2, self.a_wvt)):
            for dc in range(16):
                self.dmat("sync", "dWT", dst[:, dc * 512:(dc + 1) * 512],
                          self.wqkv_scr[p3][0:512, dc * 128:(dc + 1) * 128])
        bank_i = 0
        for eh in range(8):
            buf = eh % 2
            hT = self.a_hT[:, buf * 8192:(buf + 1) * 8192]
            if eh >= 2:
                self.wt(["sync"], "sT", self.mark[f"proj_T_{eh - 2}"])
            for dc in range(16):
                self.dmat("sync", f"dHT{buf}", hT[:, dc * 512:(dc + 1) * 512],
                          self.hag_out[eh * 512:(eh + 1) * 512, dc * 128:(dc + 1) * 128])
            self.mark[f"hT_L_{eh}"] = self.cnt[f"dHT{buf}"]
            self.wt(["tensor"], f"dHT{buf}", self.mark[f"hT_L_{eh}"])
            if eh == 0:
                self.wt(["tensor"], "dWT")
            for wt_, dst in ((self.a_wqt, self.a_qt), (self.a_wkt, self.a_kt)):
                for h in range(HPC):
                    b = self.PB[bank_i % 6]
                    bank_i += 1
                    self.bank_acquire(b)
                    for dc in range(16):
                        self.tinc("sT", self.e.matmul(
                            b.ap[:, :], wt_[:, dc * 512 + h * 128: dc * 512 + (h + 1) * 128],
                            hT[:, dc * 512:(dc + 1) * 512],
                            start=(dc == 0), stop=(dc == 15)) if self.on("tensor") else None,
                            k=(1 if dc == 15 else 0))
                    self.wt(["vector"], "sT")
                    self.vop("tensor_copy",
                             dst[:, h * S + eh * 512: h * S + (eh + 1) * 512], b.ap[:, :])
                    self.bank_release(b, "sV")
            for kt4 in range(4):
                b = self.PB[bank_i % 6]
                bank_i += 1
                self.bank_acquire(b)
                for dc in range(16):
                    self.tinc("sT", self.e.matmul(
                        b.ap[:, :], hT[:, dc * 512 + kt4 * 128: dc * 512 + (kt4 + 1) * 128],
                        self.a_wvt[:, dc * 512:(dc + 1) * 512],
                        start=(dc == 0), stop=(dc == 15)) if self.on("tensor") else None,
                        k=(1 if dc == 15 else 0))
                self.wt(["vector"], "sT")
                self.vop("tensor_copy",
                         self.a_v[:, (eh * 4 + kt4) * 512:(eh * 4 + kt4 + 1) * 512], b.ap[:, :])
                self.bank_release(b, "sV")
            self.mark[f"proj_T_{eh}"] = self.cnt["sT"]
        self.mark["proj_V"] = self.cnt["sV"]
        if "hT" in self.tap_t:
            self.wt(["sync"], "sT", self.mark["proj_T_7"])
            self.dma("sync", "sO", self.tap_t["hT"], self.a_hT[:, 0:8192])
        if "wqt" in self.tap_t:
            self.wt(["sync"], "sT", self.mark["proj_T_7"])
            self.dma("sync", "sO", self.tap_t["wqt"], self.a_wqt[:, :])
        self._tap_proj()

    def _tap_proj(self):
        if "qt" in self.tap_t:
            self.wt(["sync"], "sV", self.mark["proj_V"])
            for h in range(HPC):
                self.dma("sync", "sO", self.tap_t["qt"][h * E:(h + 1) * E, :],
                         self.a_qt[:, h * S:(h + 1) * S])
                self.dma("sync", "sO", self.tap_t["kt"][h * E:(h + 1) * E, :],
                         self.a_kt[:, h * S:(h + 1) * S])
        if "v" in self.tap_t:
            self.wt(["sync"], "sV", self.mark["proj_V"])
            for kt4 in range(32):
                self.dma("sync", "sO", self.tap_t["v"][kt4 * 128:(kt4 + 1) * 128, :],
                         self.a_v[:, kt4 * 512:(kt4 + 1) * 512])

    def ph3_attn(self):
        hrange = 1 if self.sub < 90 else HPC
        qtrange = 4 if self.sub < 90 else 32
        for h in range(hrange):
            for qt in range(qtrange):
                klen = (qt + 1) * 128
                nkb = (klen + 511) // 512
                qcol = h * S + qt * 128
                for kb in range(nkb):
                    w = min(512, klen - kb * 512)
                    bS = self.PB[kb % 2]
                    self.bank_acquire(bS)
                    self.tinc("sT", self.e.matmul(
                        bS.ap[:, 0:w], self.a_qt[:, qcol:qcol + 128],
                        self.a_kt[:, h * S + kb * 512: h * S + kb * 512 + w],
                        start=True, stop=True) if self.on("tensor") else None)
                    last = (kb == nkb - 1)
                    if self.sub == 6:
                        self.wt(["vector"], "sT")
                        self.vop("tensor_copy", self.a_sS[:, kb * 512: kb * 512 + w], bS.ap[:, 0:w])
                        self.bank_release(bS, "sV")
                        continue
                    if w > 128:
                        self.wt(["scalar"], "sT")
                        self.sop_d("copy", self.a_sS[:, kb * 512: kb * 512 + (w - 128 if last else w)],
                                   bS.ap[:, 0:(w - 128 if last else w)])
                        self.mark["sc_evac"] = self.cnt["sS"]
                    if self.sub == 7:
                        if w > 128:
                            self.bank_release(bS, "sS")
                        else:
                            self.wt(["vector"], "sT")
                            self.vop("tensor_copy", self.a_sS[:, kb * 512: kb * 512 + w], bS.ap[:, 0:w])
                            self.bank_release(bS, "sV")
                        continue
                    if last:
                        self.wt(["vector"], "sT")
                        if w > 128:
                            self.wt(["vector"], "sS", self.mark["sc_evac"])
                        self.vop("scalar_tensor_tensor", self.a_sS[:, klen - 128:klen],
                                 bS.ap[:, w - 128:w], 1.0, self.s_amask[:, :],
                                 ALU.mult, ALU.add)
                        self.bank_release(bS, "sV")
                        if w > 128:
                            self.bank_release(bS, "sS", add=True)
                    else:
                        self.bank_release(bS, "sS")
                    if self.sub == 8:
                        continue
                    if w > 128:
                        self.wt(["vector"], "sS")
                    self.vop("tensor_reduce", self.a_mx[:, kb:kb + 1],
                             self.a_sS[:, kb * 512: kb * 512 + w], AX.X, ALU.max)
                if self.sub in (2, 6, 7, 8):
                    continue
                self.dr("vector")
                if self.on("vector"):
                    self.e.tensor_reduce(self.a_mx[:, 8:9], self.a_mx[:, 0:nkb], AX.X, ALU.max)
                    self.e.drain()
                    self.e.tensor_scalar(self.a_mx[:, 9:10], self.a_mx[:, 8:9], -C_SM, None, ALU.mult)
                    self.e.drain()
                self.vop("nop")
                self.mark["sm_m"] = self.cnt["sV"]
                self.wt(["scalar"], "sV", self.mark["sm_m"])
                if "pfree" in self.mark:
                    self.wt(["scalar"], "sV", self.mark["pfree"])
                self.sop_d("activation", self.a_P[:, 0:klen], self.a_sS[:, 0:klen], AF.Exp,
                           bias=self.a_mx[:, 9:10], scale=C_SM, accum_out=self.a_mx[:, 10:11])
                self.mark["sm_e"] = self.cnt["sS"]
                self.wt(["vector"], "sS", self.mark["sm_e"])
                if self.on("vector"):
                    self.e.reciprocal(self.a_mx[:, 11:12], self.a_mx[:, 10:11])
                    self.e.drain()
                self.vop("nop")
                if "ptT_T" in self.mark:
                    self.wt(["vector"], "sT", self.mark["ptT_T"])
                self.vop("tensor_scalar", self.a_Pn[:, 0:klen], self.a_P[:, 0:klen],
                         self.a_mx[:, 11:12], None, ALU.mult)
                self.mark["pfree"] = self.cnt["sV"]
                self.mark["pn_v"] = self.cnt["sV"]
                if self.sub == 3:
                    continue
                nbat = (qt + 1 + 7) // 8
                for bt_ in range(nbat):
                    n8 = min(8, qt + 1 - bt_ * 8)
                    self.wt(["tensor"], "sV", self.mark["pn_v"])
                    self.bank_acquire(self.PT)
                    for j in range(n8):
                        kc = bt_ * 8 + j
                        self.tinc("sT", self.e.transpose(
                            self.PT.ap[:, j * 128:(j + 1) * 128],
                            self.a_Pn[:, kc * 128:(kc + 1) * 128], self.s_ident[:, :])
                            if self.on("tensor") else None)
                    self.wt(["vector"], "sT")
                    self.vop("tensor_copy", self.a_ptT[:, bt_ * 1024: bt_ * 1024 + n8 * 128],
                             self.PT.ap[:, 0:n8 * 128])
                    self.bank_release(self.PT, "sV")
                self.mark["ptT_v"] = self.cnt["sV"]
                if self.sub == 4:
                    continue
                bz = self.PB[4 + qt % 2]
                self.bank_acquire(bz)
                self.wt(["tensor"], "sV", self.mark["ptT_v"])
                for kc in range(qt + 1):
                    self.tinc("sT", self.e.matmul(
                        bz.ap[:, 0:128],
                        self.a_v[:, kc * 512 + h * 128: kc * 512 + (h + 1) * 128],
                        self.a_ptT[:, kc * 128:(kc + 1) * 128],
                        start=(kc == 0), stop=(kc == qt)) if self.on("tensor") else None,
                        k=(1 if kc == qt else 0))
                self.mark["ptT_T"] = self.cnt["sT"]
                self.wt(["vector"], "sT")
                strip = self.a_zb[:, (qt // 4 % 2) * 512:((qt // 4 % 2) + 1) * 512]
                if f"zst_{qt // 4 % 2}" in self.mark:
                    self.wt(["vector"], f"dZst{qt // 4 % 2}", self.mark[f"zst_{qt // 4 % 2}"])
                self.vop("tensor_copy", strip[:, (qt % 4) * 128:(qt % 4 + 1) * 128],
                         bz.ap[:, 0:128])
                self.bank_release(bz, "sV")
                if qt % 4 == 3:
                    qb = qt // 4
                    self.wt(["sync"], "sV")
                    self.dma("sync", f"dZst{qb % 2}",
                             self.zag_in[h * E:(h + 1) * E, qb * 512:(qb + 1) * 512], strip)
                    self.mark[f"zst_{qb % 2}"] = self.cnt[f"dZst{qb % 2}"]
        self.mark["z_V"] = self.cnt["sV"]
        self.wt(["gpsimd"], "dZst0")
        self.wt(["gpsimd"], "dZst1")
        self.tinc("sCC", self.e.collective_compute(
            "AllGather", ALU.bypass, replica_groups=RG4,
            ins=[self.zag_in], outs=[self.zag_out]) if self.on("gpsimd") else None)
        self.mark["z_ready"] = self.cnt["sCC"]
        if "z" in self.tap_t:
            self.wt(["sync"], "sCC", self.mark["z_ready"])
            self.dma("sync", "sO", self.tap_t["z"], self.zag_in)

    def _own_q(self):
        if self._qreg is None:
            pid = self.e.partition_id()
            self._qreg = self.e.snap(pid % 4, min_val=0, max_val=3)
        return self._qreg

    def ph4_post(self):
        st = self.s_lnst
        self.wt(["sync"], "sCC", self.mark["z_ready"])
        for h in range(16):
            self.dmat("sync", "dWot", self.c_wot[:, h * D:(h + 1) * D], self.wo_scr[h])
        self.mark["wot_L"] = self.cnt["dWot"]

        for qt in range(8):
            zbuf = qt % 2
            zdst = self.c_zsl.rearrange("p (b h f) -> p b h f", b=2, h=16)[:, zbuf]
            if qt >= 2:
                self.wt(["sync"], "sT", self.mark[f"op_T_{qt - 2}"])
            if self.on("sync"):
                src = self.zag_out.rearrange("(h p) s -> p h s", p=128)[
                    :, :, bass.ds(self._own_q() * TPC + qt * 128, 128)]
            else:
                src = self.zag_out.rearrange("(h p) s -> p h s", p=128)[:, :, 0:128]
            self.dma("sync", f"dZsl{zbuf}", zdst, src)
            self.mark[f"zsl_L_{qt}"] = self.cnt[f"dZsl{zbuf}"]
            xrb = self.c_xr[:, :]
            if qt >= 1:
                self.wt(["scalar"], "sV", self.mark[f"x1_{qt - 1}"])
            self.wt(["sync"], "sV", self.mark[f"x1_{qt - 1}"] if qt >= 1 else self.mark["z_V"])
            self.dma("sync", "dXr", xrb, self.xb_own[qt * 128:(qt + 1) * 128, :])
            self.mark[f"xr_L_{qt}"] = self.cnt["dXr"]
            self.wt(["tensor"], "dWot", self.mark["wot_L"])
            self.wt(["tensor"], f"dZsl{zbuf}", self.mark[f"zsl_L_{qt}"])
            bs = [self.PB[j] for j in range(4)]
            for b in bs:
                self.bank_acquire(b)
            for j in range(4):
                for h in range(16):
                    self.tinc("sT", self.e.matmul(
                        bs[j].ap[:, :],
                        self.c_zsl[:, zbuf * 2048 + h * 128: zbuf * 2048 + (h + 1) * 128],
                        self.c_wot[:, h * D + j * 512: h * D + (j + 1) * 512],
                        start=(h == 0), stop=(h == 15)) if self.on("tensor") else None,
                        k=(1 if h == 15 else 0))
            self.mark[f"op_T_{qt}"] = self.cnt["sT"]
            self.wt(["vector"], "sT")
            for j in range(4):
                self.vop("tensor_copy", self.c_ao[:, j * 512:(j + 1) * 512], bs[j].ap[:, :])
                self.bank_release(bs[j], "sV")
            if "ao" in self.tap_t:
                self.wt(["sync"], "sV")
                self.dma("sync", "sO", self.tap_t["ao"][qt * 128:(qt + 1) * 128, :], self.c_ao[:, :])
            if "zsl" in self.tap_t:
                self.wt(["sync"], f"dZsl{zbuf}", self.mark[f"zsl_L_{qt}"])
                self.dma("sync", "sO", self.tap_t["zsl"][qt * 128:(qt + 1) * 128, :],
                         self.c_zsl[:, zbuf * 2048:(zbuf + 1) * 2048])
            self.dr("vector")
            self._ln_stats(self.c_ao)
            self.wt(["vector"], "dXr", self.mark[f"xr_L_{qt}"])
            if self.on("vector"):
                self.e.tensor_scalar(self.c_u[:, :], self.c_ao[:, :], st[:, 0:1], st[:, 1:2],
                                     ALU.subtract, ALU.mult)
                self.e.drain()
                self.e.tensor_tensor(self.c_x1[:, :], self.c_u[:, :], xrb, ALU.add)
                self.e.drain()
            self.vop("nop")
            self.mark[f"x1_{qt}"] = self.cnt["sV"]
            if "xr" in self.tap_t:
                self.wt(["sync"], "sV", self.mark[f"x1_{qt}"])
                self.dma("sync", "sO", self.tap_t["xr"][qt * 128:(qt + 1) * 128, :], xrb)
            self.wt(["sync"], "sV", self.mark[f"x1_{qt}"])
            self.dma("sync", "dX1", self.x1buf[qt * 128:(qt + 1) * 128, :], self.c_x1[:, :])
            if "x1" in self.tap_t:
                self.dma("sync", "sO", self.tap_t["x1"][qt * 128:(qt + 1) * 128, :],
                         self.c_x1[:, :])
            self._ln_stats(self.c_x1)
            if self.on("vector"):
                self.e.tensor_scalar(self.c_h2[:, :], self.c_x1[:, :], st[:, 0:1], st[:, 1:2],
                                     ALU.subtract, ALU.mult)
                self.e.drain()
                self.e.reduce_max(self.s_small[:, 0:1], self.c_h2[:, :], AX.X,
                                  apply_absolute_value=True)
                self.e.drain()
                self.e.tensor_scalar(self.s_small[:, 1:2], self.s_small[:, 0:1], EPS_Q,
                                     None, ALU.max)
                self.e.drain()
                self.e.reciprocal(self.s_small[:, 2:3], self.s_small[:, 1:2])
                self.e.tensor_scalar(self.s_small[:, 4:5], self.s_small[:, 1:2], INV128,
                                     None, ALU.mult)
                self.e.drain()
                self.e.tensor_scalar(self.s_small[:, 3:4], self.s_small[:, 2:3], 128.0,
                                     None, ALU.mult)
                self.e.drain()
                self.e.tensor_scalar(self.c_u[:, :], self.c_h2[:, :], self.s_small[:, 3:4],
                                     MAGIC, ALU.mult, ALU.add)
                self.e.drain()
                self.e.tensor_scalar(self.c_ao[:, :], self.c_u[:, :], MAGIC, -128.0,
                                     ALU.subtract, ALU.max)
                self.e.drain()
                self.e.tensor_scalar(self.c_xq[:, :], self.c_ao[:, :], 127.0,
                                     self.s_small[:, 4:5], ALU.min, ALU.mult)
                self.e.drain()
            self.vop("nop")
            self.mark[f"xq_{qt}"] = self.cnt["sV"]
            if "xq" in self.tap_t:
                self.wt(["sync"], "sV", self.mark[f"xq_{qt}"])
                self.dma("sync", "sO", self.tap_t["xq"][qt * 128:(qt + 1) * 128, :],
                         self.c_xq[:, :])
            for half in range(2):
                self.wt(["tensor"], "sV", self.mark[f"xq_{qt}"])
                self.bank_acquire(self.PT)
                for c8 in range(8):
                    dc = half * 8 + c8
                    self.tinc("sT", self.e.transpose(
                        self.PT.ap[:, c8 * 128:(c8 + 1) * 128],
                        self.c_xq[:, dc * 128:(dc + 1) * 128], self.s_ident[:, :])
                        if self.on("tensor") else None)
                self.wt(["vector"], "sT")
                outap = self.c_xqT.rearrange("p (dc t) -> p dc t", dc=16)[
                    :, half * 8:(half + 1) * 8, qt * 128:(qt + 1) * 128]
                self.vop("tensor_copy", outap, self.PT.ap[:, :])
                self.bank_release(self.PT, "sV")
        self.mark["xqT_V"] = self.cnt["sV"]
        self.mark["ph4_T"] = self.cnt["sT"]
        self.mark["x1st"] = self.cnt["dX1"]

    def _ln_stats(self, src_f32):
        st = self.s_lnst
        if self.on("vector"):
            self.e.reduce_sum(st[:, 2:3], src_f32[:, :], AX.X)
            self.e.drain()
            self.e.scalar_tensor_tensor(self.c_u[:, :], src_f32[:, :], 1.0, src_f32[:, :],
                                        ALU.mult, ALU.mult, accum_out=st[:, 3:4])
            self.e.drain()
            self.e.tensor_scalar(st[:, 0:1], st[:, 2:3], INV2048, None, ALU.mult)
            self.e.tensor_scalar(st[:, 4:5], st[:, 3:4], INV2048, None, ALU.mult)
            self.e.drain()
            self.e.tensor_tensor(st[:, 5:6], st[:, 0:1], st[:, 0:1], ALU.mult)
            self.e.drain()
            self.e.scalar_tensor_tensor(st[:, 6:7], st[:, 4:5], 1.0, st[:, 5:6],
                                        ALU.mult, ALU.subtract)
            self.e.drain()
            self.e.tensor_scalar(st[:, 7:8], st[:, 6:7], EPS_LN, None, ALU.add)
            self.e.drain()
            self.e.reciprocal(st[:, 8:9], st[:, 7:8])
            self.e.drain()
        self.vop("nop")
        self.wt(["scalar"], "sV")
        self.sop_d("activation", st[:, 1:2], st[:, 8:9], AF.Sqrt)
        self.wt(["vector"], "sS")

    def ph6_ffn(self):
        for th in range(2):
            self._ffn_gateup(th)
            self._ffn_fwht(th)
            self._ffn_actq2(th)
            self._ffn_down(th)

    def _ffn_gateup(self, th):
        self.wt(["scalar"], "sCC", self.mark["t_ready"])
        for g in range(32):
            buf = g % 2
            rank = (g * 256) // 1024
            col = (g * 256) % 1024
            for tout, tdst in ((self.tg_out, self.f_tg), (self.tu_out, self.f_tu)):
                if g >= 2:
                    self.wt(["sync"], "sT", self.mark[f"gu_T_{th}_{g - 2}"])
                elif th == 1:
                    self.wt(["sync"], "sT", self.mark["dn_T_0_15"])
                else:
                    self.wt(["sync"], "sT", self.mark["ph4_T"])
                    self.wt(["sync"], "dX1", self.mark["x1st"])
                src = tout[rank * D:(rank + 1) * D, col:col + 256] \
                    .rearrange("(dc p) f -> p dc f", p=128)
                self.dma("sync", f"dTg{buf}",
                         tdst[:, buf * 4096:(buf + 1) * 4096].rearrange("p (dc f) -> p dc f", dc=16),
                         src)
            self.mark[f"gu_L_{th}_{g}"] = self.cnt[f"dTg{buf}"]
            bg = [self.PB[0], self.PB[1]] if g % 2 == 0 else [self.PB[2], self.PB[3]]
            bu = [self.PB[4], self.PB[5]] if g % 2 == 0 else [self.PB[2], self.PB[3]]
            self.wt(["tensor"], f"dTg{buf}", self.mark[f"gu_L_{th}_{g}"])
            if th == 0 and g == 0:
                self.wt(["tensor"], "sV", self.mark["xqT_V"])
            for ic in range(2):
                self.bank_acquire(bg[ic])
                for dc in range(16):
                    self.tinc("sT", self.e.matmul(
                        bg[ic].ap[:, :],
                        self.f_tg[:, buf * 4096 + dc * 256 + ic * 128:
                                  buf * 4096 + dc * 256 + (ic + 1) * 128],
                        self.c_xqT[:, dc * TPC + th * 512: dc * TPC + (th + 1) * 512],
                        start=(dc == 0), stop=(dc == 15)) if self.on("tensor") else None,
                        k=(1 if dc == 15 else 0))
            self.wt(["scalar"], "sT")
            if g >= 1:
                self.wt(["scalar"], "sV", self.mark[f"gu_V_{th}_{g - 1}"])
            for ic in range(2):
                self.sop_d("activation", self.f_sg[:, ic * 512:(ic + 1) * 512],
                           bg[ic].ap[:, :], AF.Silu, scale=self.s_wsc[:, 0:1])
                self.bank_release(bg[ic], "sS")
            for ic in range(2):
                self.bank_acquire(bu[ic])
                for dc in range(16):
                    self.tinc("sT", self.e.matmul(
                        bu[ic].ap[:, :],
                        self.f_tu[:, buf * 4096 + dc * 256 + ic * 128:
                                  buf * 4096 + dc * 256 + (ic + 1) * 128],
                        self.c_xqT[:, dc * TPC + th * 512: dc * TPC + (th + 1) * 512],
                        start=(dc == 0), stop=(dc == 15)) if self.on("tensor") else None,
                        k=(1 if dc == 15 else 0))
            self.wt(["vector"], "sT")
            self.wt(["vector"], "sS")
            for ic in range(2):
                self.vop("scalar_tensor_tensor",
                         self.f_inter[:, ic * 512:(ic + 1) * 512], bu[ic].ap[:, :],
                         self.s_wsc[:, 1:2], self.f_sg[:, ic * 512:(ic + 1) * 512],
                         ALU.mult, ALU.mult)
                self.bank_release(bu[ic], "sV")
            self.mark[f"gu_V_{th}_{g}"] = self.cnt["sV"]
            if "inter" in self.tap_t and th == 0:
                self.wt(["sync"], "sV")
                for ic in range(2):
                    c = g * 2 + ic
                    self.dma("sync", "sO", self.tap_t["inter"][c * 128:(c + 1) * 128, :],
                             self.f_inter[:, ic * 512:(ic + 1) * 512])
            self.wt(["tensor"], "sV")
            for ic in range(2):
                c = g * 2 + ic
                bb = bg[ic]
                self.bank_acquire(bb)
                self.tinc("sT", self.e.matmul(
                    bb.ap[:, :], self.s_h128[:, :], self.f_inter[:, ic * 512:(ic + 1) * 512],
                    start=True, stop=True) if self.on("tensor") else None)
                self.wt(["scalar"], "sT")
                self.sop_d("copy", self.f_z[:, c * 512:(c + 1) * 512], bb.ap[:, :])
                self.bank_release(bb, "sS")
            self.mark[f"gu_T_{th}_{g}"] = self.cnt["sT"]
        self.mark[f"z1_S_{th}"] = self.cnt["sS"]

    def _zap(self, sl):
        if sl < 64:
            return self.f_z[:, sl * 512:(sl + 1) * 512]
        return self.f_sp[:, (sl - 64) * 512:(sl - 63) * 512]

    def _ffn_fwht(self, th):
        self.wt(["vector"], "sS", self.mark[f"z1_S_{th}"])
        slot = list(range(64))
        spare = [64, 65]
        for stg in range(6):
            hstep = 1 << stg
            for c in range(64):
                c2 = c ^ hstep
                if c2 < c:
                    continue
                sa, sb = slot[c], slot[c2]
                da, db = spare
                if self.on("vector"):
                    self.e.tensor_tensor(self._zap(da), self._zap(sa), self._zap(sb), ALU.add)
                    self.e.tensor_tensor(self._zap(db), self._zap(sa), self._zap(sb), ALU.subtract)
                spare = [sa, sb]
                slot[c], slot[c2] = da, db
            self.dr("vector")
        self.vop("nop")
        self.mark[f"fwht_V_{th}"] = self.cnt["sV"]
        self.fw_slot = list(slot)
        if "zz" in self.tap_t and th == 0:
            self.wt(["sync"], "sV", self.mark[f"fwht_V_{th}"])
            for c in range(64):
                self.dma("sync", "sO", self.tap_t["zz"][c * 128:(c + 1) * 128, :],
                         self._zap(slot[c]))

    def _ffn_actq2(self, th):
        slot = self.fw_slot
        self.wt(["scalar"], "sV", self.mark[f"fwht_V_{th}"])
        self.wt(["scalar"], "sO")
        for k in range(3):
            self.sop_d("activation", self.f_macc[:, k * 512:(k + 1) * 512],
                       self._zap(slot[k]), AF.Abs)
        self.mark[f"aq2_init_{th}"] = self.cnt["sS"]
        self.wt(["vector"], "sS", self.mark[f"aq2_init_{th}"])
        par_of = [0, 0, 0]
        for c in range(3, 64):
            ab = self.f_abs[:, (c % 2) * 512:((c % 2) + 1) * 512]
            if c >= 5:
                self.wt(["scalar"], "sV", self.mark[f"aq2v_{th}_{c - 2}"])
            self.sop_d("activation", ab, self._zap(slot[c]), AF.Abs)
            self.wt(["vector"], "sS")
            k = c % 3
            p = par_of[k]
            src_acc = self.f_macc[:, (k + 3 * p) * 512:(k + 3 * p + 1) * 512]
            dst_acc = self.f_macc[:, (k + 3 * (1 - p)) * 512:(k + 3 * (1 - p) + 1) * 512]
            self.vop("tensor_max", dst_acc, src_acc, ab)
            par_of[k] = 1 - p
            self.mark[f"aq2v_{th}_{c}"] = self.cnt["sV"]
        self.dr("vector")
        accs = [self.f_macc[:, (k + 3 * par_of[k]) * 512:(k + 3 * par_of[k] + 1) * 512]
                for k in range(3)]
        free0 = self.f_macc[:, (0 + 3 * (1 - par_of[0])) * 512:(0 + 3 * (1 - par_of[0]) + 1) * 512]
        self.vop("tensor_max", free0, accs[0], accs[1])
        self.dr("vector")
        macc = self.f_macc[:, (1 + 3 * (1 - par_of[1])) * 512:(1 + 3 * (1 - par_of[1]) + 1) * 512]
        self.vop("tensor_max", macc, free0, accs[2])
        self.dr("vector")
        self.mark[f"macc_{th}"] = self.cnt["sV"]
        bnk = self.PB[0]
        self.wt(["tensor"], "sV", self.mark[f"macc_{th}"])
        self.bank_acquire(bnk)
        for t4 in range(4):
            self.tinc("sT", self.e.transpose(
                bnk.ap[:, t4 * 128:(t4 + 1) * 128], macc[:, t4 * 128:(t4 + 1) * 128],
                self.s_ident32[:, :]) if self.on("tensor") else None)
        self.wt(["vector"], "sT")
        for t4 in range(4):
            self.vop("reduce_max", self.f_mtok[:, th * 8 + t4: th * 8 + t4 + 1],
                     bnk.ap[:, t4 * 128:(t4 + 1) * 128], AX.X)
        self.bank_release(bnk, "sV")
        self.dr("vector")
        mt = self.f_mtok
        b0 = th * 8
        if self.on("vector"):
            self.e.tensor_scalar(self.s_small[:, 16:20],
                                 mt[:, b0:b0 + 4], C_FW, None, ALU.mult)
            self.e.drain()
            self.e.tensor_scalar(self.s_small[:, 20:24], self.s_small[:, 16:20], EPS_Q,
                                 None, ALU.max)
            self.e.drain()
            self.e.reciprocal(self.s_small[:, 24:28], self.s_small[:, 20:24])
            self.e.tensor_scalar(self.s_small[:, 32:36], self.s_small[:, 20:24], INV128,
                                 None, ALU.mult)
            self.e.drain()
            self.e.tensor_scalar(self.s_small[:, 28:32], self.s_small[:, 24:28], SQRT2,
                                 None, ALU.mult)
            self.e.drain()
        self.vop("nop")
        self.mark[f"aqs_{th}"] = self.cnt["sV"]
        self.wt(["tensor"], "sV", self.mark[f"aqs_{th}"])
        b2 = self.PB[1]
        self.bank_acquire(b2)
        self.tinc("sT", self.e.transpose(b2.ap[0:4, 0:128], self.s_small[:, 28:32],
                                         self.s_ident32[:, :]) if self.on("tensor") else None)
        self.tinc("sT", self.e.transpose(b2.ap[0:4, 128:256], self.s_small[:, 32:36],
                                         self.s_ident32[:, :]) if self.on("tensor") else None)
        self.wt(["vector"], "sT")
        self.vop("tensor_copy", self.s_r4[0:4, 0:128], b2.ap[0:4, 0:128])
        self.vop("tensor_copy", self.s_r4[0:4, 128:256], b2.ap[0:4, 128:256])
        self.bank_release(b2, "sV")
        self.wt(["sync"], "sV")
        for seg in range(4):
            self.dma("sync", "sL", self.s_row[0:1, 8 + seg * 128: 8 + (seg + 1) * 128],
                     self.s_r4[seg:seg + 1, 0:128])
        self.mark[f"scs_row_{th}"] = self.cnt["sL"]
        self.wt(["vector"], "sL", self.mark[f"scs_row_{th}"])
        self.bcast_row(self.s_row[0:1, 8:520], self.s_rhi[0:1, 8:520], self.s_rlo[0:1, 8:520],
                       self.PX, 512, exact=True)
        self.wt(["vector"], "sT")
        self.vop("tensor_copy", self.f_scs[:, :], self.PX.ap[:, :])
        self.bank_release(self.PX, "sV")
        self.wt(["sync"], "sV")
        for seg in range(4):
            self.dma("sync", "sL", self.s_row[0:1, 8 + seg * 128: 8 + (seg + 1) * 128],
                     self.s_r4[seg:seg + 1, 128:256])
        self.mark[f"invs_row_{th}"] = self.cnt["sL"]
        self.wt(["vector"], "sL", self.mark[f"invs_row_{th}"])
        self.bcast_row(self.s_row[0:1, 8:520], self.s_rhi[0:1, 8:520], self.s_rlo[0:1, 8:520],
                       self.PX, 512, exact=True)
        self.wt(["vector"], "sT")
        self.vop("tensor_copy", self.f_invs[:, :], self.PX.ap[:, :])
        self.bank_release(self.PX, "sV")
        self.dr("vector")
        self.mark[f"aqb_{th}"] = self.cnt["sV"]

        self.wt(["vector"], "sV", self.mark[f"aqb_{th}"])
        self.wt(["scalar"], "sV", self.mark[f"aqb_{th}"])
        NC = 64
        for step in range(NC + 2):
            if step < NC:
                c = step
                self.vop("tensor_tensor", self.f_uq[:, (c % 2) * 512:((c % 2) + 1) * 512],
                         self._zap(slot[c]), self.f_scs[:, :], ALU.mult)
                self.mark[f"qA_{th}_{c}"] = self.cnt["sV"]
            if 1 <= step <= NC:
                c = step - 1
                self.wt(["scalar"], "sV", self.mark[f"qA_{th}_{c}"])
                self.sop("activation", self.f_abs[:, (c % 2) * 512:((c % 2) + 1) * 512],
                         self.f_uq[:, (c % 2) * 512:((c % 2) + 1) * 512], AF.Identity,
                         bias=self.s_magic[:, 0:1])
                self.dr("scalar")
                self.sop_d("activation", self.f_rq[:, (c % 2) * 512:((c % 2) + 1) * 512],
                           self.f_abs[:, (c % 2) * 512:((c % 2) + 1) * 512], AF.Identity,
                           bias=self.s_magic[:, 1:2])
                self.mark[f"qB_{th}_{c}"] = self.cnt["sS"]
            if step >= 2:
                c = step - 2
                self.wt(["vector"], "sS", self.mark[f"qB_{th}_{c}"])
                self.vop("tensor_scalar", self.f_qq[:, 0:512],
                         self.f_rq[:, (c % 2) * 512:((c % 2) + 1) * 512], -128.0, 127.0,
                         ALU.max, ALU.min)
                self.dr("vector")
                self.vop("tensor_tensor", self._zap(slot[c]),
                         self.f_qq[:, 0:512],
                         self.f_invs[:, :], ALU.mult)
                self.mark[f"qD_{th}_{c}"] = self.cnt["sV"]
        self.mark[f"xq2_{th}"] = self.cnt["sV"]
        if "xq2" in self.tap_t and th == 0:
            self.wt(["sync"], "sV", self.mark[f"xq2_{th}"])
            for c in range(64):
                self.dma("sync", "sO", self.tap_t["xq2"][c * 128:(c + 1) * 128, :],
                         self._zap(slot[c]))

    def _ffn_down(self, th):
        slot = self.fw_slot
        st = self.s_lnst
        for dc in range(16):
            buf = dc % 2
            if dc >= 2:
                self.wt(["sync"], "sT", self.mark[f"dn_T_{th}_{dc - 2}"])
            else:
                self.wt(["sync"], "sT", self.mark[f"gu_T_{th}_31"])
            rank = dc // 2
            src = self.td_out[rank * II: (rank + 1) * II, (dc % 2) * 128:(dc % 2 + 1) * 128] \
                .rearrange("(ic p) f -> p ic f", p=128)
            self.dma("sync", f"dDs{buf}",
                     self.f_dstream[:, buf * 8192:(buf + 1) * 8192].rearrange("p (ic f) -> p ic f", ic=64),
                     src)
            self.mark[f"dn_L_{th}_{dc}"] = self.cnt[f"dDs{buf}"]
            b = self.PB[2 + dc % 2]
            self.wt(["tensor"], f"dDs{buf}", self.mark[f"dn_L_{th}_{dc}"])
            self.wt(["tensor"], "sV", self.mark[f"xq2_{th}"])
            self.bank_acquire(b)
            for Ic in range(64):
                self.tinc("sT", self.e.matmul(
                    b.ap[:, :], self.f_dstream[:, buf * 8192 + Ic * 128: buf * 8192 + (Ic + 1) * 128],
                    self._zap(slot[Ic]), start=(Ic == 0), stop=(Ic == 63))
                    if self.on("tensor") else None, k=(1 if Ic == 63 else 0))
            self.mark[f"dn_T_{th}_{dc}"] = self.cnt["sT"]
            self.wt(["scalar"], "sT", self.mark[f"dn_T_{th}_{dc}"])
            self.sop_d("activation", self.f_ffT[:, :], b.ap[:, :], AF.Identity,
                       scale=self.s_wsc[:, 2:3])
            self.bank_release(b, "sS")
            self.mark[f"dn_S_{th}_{dc}"] = self.cnt["sS"]
            self.wt(["tensor"], "sS", self.mark[f"dn_S_{th}_{dc}"])
            self.bank_acquire(self.PT)
            for t4 in range(4):
                self.tinc("sT", self.e.transpose(
                    self.PT.ap[:, t4 * 128:(t4 + 1) * 128],
                    self.f_ffT[:, t4 * 128:(t4 + 1) * 128], self.s_ident[:, :])
                    if self.on("tensor") else None)
            self.wt(["vector"], "sT")
            if "ffst" in self.mark:
                self.wt(["vector"], "dFst", self.mark["ffst"])
            self.vop("tensor_copy", self.f_ffT2[:, :], self.PT.ap[:, 0:512])
            self.bank_release(self.PT, "sV")
            self.wt(["sync"], "sV")
            dstap = self.ffnbuf[th * 512:(th + 1) * 512, dc * 128:(dc + 1) * 128] \
                .rearrange("(t4 p) f -> p t4 f", p=128)
            self.dma("sync", "dFst", dstap, self.f_ffT2.rearrange("p (t4 f) -> p t4 f", t4=4))
            self.mark["ffst"] = self.cnt["dFst"]
        self.mark[f"ffn_st_{th}"] = self.cnt["dFst"]
        if "ffn" in self.tap_t:
            self.wt(["sync"], "dFst", self.mark[f"ffn_st_{th}"])
            self.dma("sync", "sO", self.tap_t["ffn"][th * 512:(th + 1) * 512, :],
                     self.ffnbuf[th * 512:(th + 1) * 512, :])
        for t4 in range(4):
            qt = th * 4 + t4
            xrb = self.c_xr[:, :]
            self.wt(["sync"], "sV")
            self.dma("sync", "dXr", xrb, self.x1buf[qt * 128:(qt + 1) * 128, :])
            self.mark[f"x1r_L_{qt}"] = self.cnt["dXr"]
            self.wt(["gpsimd"], "dFst", self.mark[f"ffn_st_{th}"])
            if t4 >= 1:
                self.wt(["gpsimd"], "sV", self.mark[f"ln4_{th}_{t4 - 1}"])
            self.dma("gpsimd", "dFr", self.c_h2[:, :], self.ffnbuf[qt * 128:(qt + 1) * 128, :])
            self.wt(["vector"], "dFr")
            self._ln_stats(self.c_h2)
            self.wt(["vector"], "dXr", self.mark[f"x1r_L_{qt}"])
            if self.on("vector"):
                self.e.tensor_scalar(self.c_u[:, :], self.c_h2[:, :], st[:, 0:1], st[:, 1:2],
                                     ALU.subtract, ALU.mult)
                self.e.drain()
                self.e.tensor_tensor(self.c_ao[:, :], self.c_u[:, :], xrb, ALU.add)
                self.e.drain()
            self.vop("nop")
            self.mark[f"ln4_{th}_{t4}"] = self.cnt["sV"]
            self.wt(["sync"], "sV")
            self.dma("sync", "sO", self.out[qt * 128:(qt + 1) * 128, :], self.c_ao[:, :])

    def fin(self):
        self.wt(["gpsimd"], "sO")


def build(stage=6, taps=(), sub=99):
    nc = bass.Bass("TRN2", target_bir_lowering=False)
    kb = KB(nc, stage=stage, taps=taps, sub=sub)
    kb.build()
    return nc, kb


_B, _S, _D = 2, 4096, 2048


def kernel(L0_attn_x, L0_attn_W_Q, L0_attn_W_K, L0_attn_W_V, L0_attn_W_O,
           L0_ffn_w_gate, L0_ffn_w_up, L0_ffn_w_down):
    from concourse.bass_utils import run_bass_kernel_spmd
    x = np.ascontiguousarray(np.asarray(L0_attn_x, dtype=np.float32))
    wqf = np.asarray(L0_attn_W_Q, dtype=np.float32)
    wkf = np.asarray(L0_attn_W_K, dtype=np.float32)
    wvf = np.asarray(L0_attn_W_V, dtype=np.float32)
    wof = np.ascontiguousarray(np.asarray(L0_attn_W_O, dtype=np.float32))
    wgf = np.asarray(L0_ffn_w_gate, dtype=np.float32)
    wuf = np.asarray(L0_ffn_w_up, dtype=np.float32)
    wdf = np.asarray(L0_ffn_w_down, dtype=np.float32)

    nc, kb = build(stage=6, taps=())
    in_maps = []
    for c in range(8):
        b, qq = c // 4, c % 4
        in_maps.append({
            "xb_own": np.ascontiguousarray(x[b, qq * TPC:(qq + 1) * TPC]),
            "wq": np.ascontiguousarray(wqf[4 * qq:4 * qq + 4]),
            "wk": np.ascontiguousarray(wkf[4 * qq:4 * qq + 4]),
            "wv": np.ascontiguousarray(wvf[4 * qq:4 * qq + 4]),
            "wo": wof,
            "wg_sl": np.ascontiguousarray(wgf[c * 1024:(c + 1) * 1024]),
            "wu_sl": np.ascontiguousarray(wuf[c * 1024:(c + 1) * 1024]),
            "wd_sl": np.ascontiguousarray(wdf[c * 256:(c + 1) * 256]),
        })
    res = run_bass_kernel_spmd(nc, in_maps, core_ids=list(range(8)))
    out = np.empty((_B, _S, _D), dtype=np.float32)
    for c in range(8):
        b, qq = c // 4, c % 4
        out[b, qq * TPC:(qq + 1) * TPC] = np.asarray(res.results[c]["out"])
    return out


# version 02 (final)
# speedup vs baseline: 1.0149x; 1.0149x over previous
import math
import numpy as np
import ml_dtypes
import concourse.bass as bass
import concourse.mybir as mybir

F32 = mybir.dt.float32
F16 = mybir.dt.float16
BF16 = mybir.dt.bfloat16
AF = mybir.ActivationFunctionType
ALU = mybir.AluOpType
AX = mybir.AxisListType

MAGIC = float(np.float32(1.5 * 2 ** 23))
C_SM = float(1.0 / math.sqrt(128.0))
C_FW = float(np.float32(1.0 / np.sqrt(np.float32(8192.0))))
SQRT2 = float(np.float32(np.float32(128.0) * np.float32(C_FW)))
INV128 = float(np.float32(1.0 / 128.0))
INV2048 = float(np.float32(1.0 / 2048.0))
INV2_24 = float(np.float32(2.0 ** -24))
EPS_LN = 1e-5
EPS_Q = 1e-5

S, D, E, HPC, TPC, II = 4096, 2048, 128, 4, 1024, 8192
RG4 = [[0, 1, 2, 3], [4, 5, 6, 7]]
RG8 = [[0, 1, 2, 3, 4, 5, 6, 7]]


def hadamard(n):
    h = np.array([[1.0]])
    while h.shape[0] < n:
        h = np.block([[h, h], [h, -h]])
    return h


class Bank:
    __slots__ = ("ap", "free")

    def __init__(self, ap):
        self.ap = ap
        self.free = []


class KB:
    def __init__(self, nc, stage=6, taps=(), sub=99):
        self.nc = nc
        self.stage = stage
        self.sub = sub
        self.taps = set(taps)
        self.cur = None
        self.e = None
        self._alloc()

    def on(self, *names):
        return self.cur in names

    def tinc(self, sem, inst, k=1):
        if inst is not None:
            inst.then_inc(self.sems[sem], k)
        self.cnt[sem] += k

    def wt(self, who, sem, val=None):
        if self.on(*who):
            self.e.wait_ge(self.sems[sem], self.cnt[sem] if val is None else val)

    def dr(self, *who):
        if self.on(*who):
            self.e.drain()

    def dma(self, queue, sem, out, in_, slow=False):
        if self.on(queue):
            self.e.dma_start(out=out, in_=in_, allow_slow_non_contiguous=slow) \
                .then_inc(self.sems[sem], 16)
        self.cnt[sem] += 16

    def dmat(self, queue, sem, out, in_):
        if self.on(queue):
            self.e.dma_start_transpose(out, in_).then_inc(self.sems[sem], 16)
        self.cnt[sem] += 16

    def bank_acquire(self, b):
        if self.on("tensor"):
            for sem, val in b.free:
                self.e.wait_ge(self.sems[sem], val)

    def bank_release(self, b, sem, add=False):
        if add:
            b.free.append((sem, self.cnt[sem]))
        else:
            b.free = [(sem, self.cnt[sem])]

    def vop(self, fname, *a, **kw):
        self.tinc("sV", getattr(self.e, fname)(*a, **kw) if self.on("vector") else None)

    def sop(self, fname, *a, **kw):
        self.tinc("sS", getattr(self.e, fname)(*a, **kw) if self.on("scalar") else None)

    def sop_d(self, fname, *a, **kw):
        if self.on("scalar"):
            getattr(self.e, fname)(*a, **kw)
            self.tinc("sS", self.e.drain())
        else:
            self.tinc("sS", None)

    def bcast_row(self, row_f32, hi, lo, bank, n, exact=True):
        if "bcast_T" in self.mark:
            self.wt(["vector"], "sT", self.mark["bcast_T"])
        self.vop("tensor_copy", hi, row_f32)
        if exact:
            self.dr("vector")
            self.vop("tensor_tensor", lo, row_f32, hi, ALU.subtract)
        self.wt(["tensor"], "sV")
        self.bank_acquire(bank)
        if exact:
            if self.on("tensor"):
                self.e.matmul(bank.ap[:, 0:n], self.s_onesrow[:, :], hi, start=True, stop=False)
            self.tinc("sT", self.e.matmul(bank.ap[:, 0:n], self.s_onesrow[:, :], lo,
                                          start=False, stop=True) if self.on("tensor") else None)
        else:
            self.tinc("sT", self.e.matmul(bank.ap[:, 0:n], self.s_onesrow[:, :], hi,
                                          start=True, stop=True) if self.on("tensor") else None)
        self.mark["bcast_T"] = self.cnt["sT"]

    def _alloc(self):
        nc = self.nc
        dt = nc.dram_tensor
        self.xb_own = dt("xb_own", [TPC, D], F32, kind="ExternalInput").ap()
        self.wq = dt("wq", [HPC, E, D], F32, kind="ExternalInput").ap()
        self.wk = dt("wk", [HPC, E, D], F32, kind="ExternalInput").ap()
        self.wv = dt("wv", [HPC, E, D], F32, kind="ExternalInput").ap()
        self.wo = dt("wo", [16, D, E], F32, kind="ExternalInput").ap()
        self.wg_sl = dt("wg_sl", [1024, D], F32, kind="ExternalInput").ap()
        self.wu_sl = dt("wu_sl", [1024, D], F32, kind="ExternalInput").ap()
        self.wd_sl = dt("wd_sl", [256, II], F32, kind="ExternalInput").ap()
        self.out = dt("out", [TPC, D], F32, kind="ExternalOutput").ap()

        f16 = np.float16
        self.c_h128 = nc.inline_tensor(hadamard(128).astype(f16), "h128")
        self.c_ident = nc.inline_tensor(np.eye(128, dtype=f16), "ident")
        self.c_ident32 = nc.inline_tensor(np.eye(128, dtype=np.float32), "ident32")
        self.c_onesrow = nc.inline_tensor(np.ones((1, 128), dtype=ml_dtypes.bfloat16), "onesrow")
        self.c_onescol = nc.inline_tensor(np.ones((128, 1), dtype=f16), "onescol")
        self.c_onescol_bf = nc.inline_tensor(np.ones((128, 1), dtype=ml_dtypes.bfloat16), "onescol_bf")
        amask = np.zeros((128, 128), dtype=np.float32)
        for q in range(128):
            amask[q, q + 1:] = -1e9
        self.c_amask = nc.inline_tensor(amask, "amask")
        self.c_magic = nc.inline_tensor(np.tile(np.array([[MAGIC, -MAGIC]], dtype=np.float32), (128, 1)), "magic")

        self.wqkv_scr = dt("wqkv_scr", [3, HPC * E, D], F16, kind="Internal").ap()
        self.wo_scr = dt("wo_scr", [16, D, E], F16, kind="Internal").ap()
        self.hag_in = dt("hag_in", [TPC, D], F16, kind="Internal").ap()
        self.hag_out = dt("hag_out", [S, D], F16, kind="Internal").ap()
        self.zag_in = dt("zag_in", [HPC * E, S], F16, kind="Internal").ap()
        self.zag_out = dt("zag_out", [16 * E, S], F16, kind="Internal").ap()
        self.ar_in = dt("ar_in", [1, 4], F32, kind="Internal").ap()
        self.ar_out = dt("ar_out", [1, 4], F32, kind="Internal", addr_space="Shared").ap()
        self.tg_in = dt("tg_in", [D, 1024], F16, kind="Internal").ap()
        self.tg_out = dt("tg_out", [8 * D, 1024], F16, kind="Internal", addr_space="Shared").ap()
        self.tu_in = dt("tu_in", [D, 1024], F16, kind="Internal").ap()
        self.tu_out = dt("tu_out", [8 * D, 1024], F16, kind="Internal", addr_space="Shared").ap()
        self.td_in = dt("td_in", [II, 256], F16, kind="Internal").ap()
        self.td_out = dt("td_out", [8 * II, 256], F16, kind="Internal", addr_space="Shared").ap()
        self.x1buf = dt("x1buf", [TPC, D], F32, kind="Internal").ap()
        self.ffnbuf = dt("ffnbuf", [TPC, D], F16, kind="Internal").ap()

        self.tap_t = {}
        tap_specs = {
            "h": ([S, D], F16), "qt": ([HPC * E, S], F16), "kt": ([HPC * E, S], F16),
            "v": ([32 * 128, HPC * E], F16), "z": ([HPC * E, S], F16),
            "x1": ([TPC, D], F32), "xq": ([TPC, D], F16), "tg": ([D, 1024], F16),
            "td": ([II, 256], F16), "biasb": ([1, 8], F32), "wsc": ([1, 8], F32),
            "inter": ([64 * 128, 512], F16), "zz": ([64 * 128, 512], F16),
            "xq2": ([64 * 128, 512], F16), "ffn": ([TPC, D], F32),
            "hT": ([128, 16 * 512], F16), "wqt": ([128, 16 * 512], F16),
            "ao": ([TPC, D], F32), "zsl": ([TPC, 16 * 128], F16), "xr": ([TPC, D], F32),
            "ssm": ([128, 64], F32), "arr": ([1, 4], F32),
        }
        for t in self.taps:
            shp, dtp = tap_specs[t]
            self.tap_t[t] = dt("tap_" + t, shp, dtp, kind="ExternalOutput").ap()

        self._off = ((nc.sbuf_base + 31) // 32) * 32

        def sb(name, shape, dtyp, at=None):
            if at is not None:
                self._off = at
            t = nc.alloc_sbuf_tensor_at(name, shape, dtyp, offset=self._off)
            nbytes = int(np.prod(shape[1:])) * np.dtype(mybir.dt.np(dtyp)).itemsize
            self._off += (nbytes + 31) // 32 * 32
            return t.ap()

        self.s_h128 = sb("s_h128", [128, 128], F16)
        self.s_ident = sb("s_ident", [128, 128], F16)
        self.s_ident32 = sb("s_ident32", [128, 128], F32)
        self.s_onesrow = sb("s_onesrow", [1, 128], BF16)
        self.s_onescol = sb("s_onescol", [128, 1], F16)
        self.s_onescol_bf = sb("s_onescol_bf", [128, 1], BF16)
        self.s_amask = sb("s_amask", [128, 128], F32)
        self.s_magic = sb("s_magic", [128, 2], F32)
        self.s_small = sb("s_small", [128, 64], F32)
        self.s_row = sb("s_row", [1, 532], F32)
        self.s_rhi = sb("s_rhi", [1, 532], BF16)
        self.s_rlo = sb("s_rlo", [1, 532], BF16)
        self.s_r4 = sb("s_r4", [128, 256], F32)
        self.s_biasB = sb("s_biasB", [128, 8], F32)
        self.s_wsc = sb("s_wsc", [128, 8], F32)
        self.s_lnst = sb("s_lnst", [128, 48], F32)
        self.s_bhl = sb("s_bhl", [128, 8], BF16)
        self.w_chunk = sb("w_chunk", [128, 2 * 1024], F32)
        self.w_zq = sb("w_zq", [128, 1024], F32)
        self.w_r = sb("w_r", [128, 1024], F32)
        self.w_t = sb("w_t", [128, 1024], F16)
        self.w_stage = sb("w_stage", [128, 1024], F16)
        pe = self._off
        self.A_BASE = AB = pe
        LIMIT = nc.sbuf_top
        K = 1024

        def at(name, shape, dtyp, off):
            t = nc.alloc_sbuf_tensor_at(name, shape, dtyp, offset=AB + off)
            nbytes = int(np.prod(shape[1:])) * np.dtype(mybir.dt.np(dtyp)).itemsize
            assert AB + off + nbytes <= LIMIT, f"{name} overflows: {AB + off + nbytes}"
            return t.ap()

        self.l_x8 = at("l_x8", [128, 8 * D], F32, 0)
        self.l_h = at("l_h", [128, 8 * D], F16, 64 * K)

        self.a_qt = at("a_qt", [128, HPC * S], F16, 0)
        self.a_kt = at("a_kt", [128, HPC * S], F16, 32 * K)
        self.a_v = at("a_v", [128, 32 * 512], F16, 64 * K)
        self.a_hT = at("a_hT", [128, 2 * 16 * 512], F16, 96 * K)
        self.a_wqt = at("a_wqt", [128, 16 * 512], F16, 128 * K)
        self.a_wkt = at("a_wkt", [128, 16 * 512], F16, 144 * K)
        self.a_wvt = at("a_wvt", [128, 16 * 512], F16, 160 * K)
        self.a_sS = at("a_sS", [128, 4096], F32, 128 * K)
        self.a_P = at("a_P", [128, 4096], F16, 144 * K)
        self.a_Pn = at("a_Pn", [128, 4096], F16, 152 * K)
        self.a_ptT = at("a_ptT", [128, 32 * 128], F16, 160 * K)
        self.a_zb = at("a_zb", [128, 2 * 512], F16, 168 * K)
        self.a_mx = at("a_mx", [128, 16], F32, 170 * K)

        self.c_wot = at("c_wot", [128, 16 * D], F16, 0)
        self.c_xqT = at("c_xqT", [128, 16 * TPC], F16, 64 * K)
        self.c_zsl = at("c_zsl", [128, 2 * 16 * 128], F16, 96 * K)
        self.c_x1 = at("c_x1", [128, D], F32, 104 * K)
        self.c_xq = at("c_xq", [128, D], F16, 112 * K)
        self.c_ao = at("c_ao", [128, D], F32, 128 * K)
        self.c_h2 = at("c_h2", [128, D], F32, 136 * K)
        self.c_u = at("c_u", [128, D], F32, 144 * K)
        self.c_xr = at("c_xr", [128, D], F32, 152 * K)

        self.f_z = at("f_z", [128, 64 * 512], F16, 0)
        self.f_tg = at("f_tg", [128, 2 * 16 * 256], F16, 96 * K)
        self.f_tu = at("f_tu", [128, 2 * 16 * 256], F16, 112 * K)
        self.f_dstream = at("f_dstream", [128, 2 * 8192], F16, 96 * K)
        self.f_sg = at("f_sg", [128, 2 * 512], F16, 160 * K)
        self.f_inter = at("f_inter", [128, 2 * 512], F16, 162 * K)
        self.f_scs = at("f_scs", [128, 512], F32, 164 * K)
        self.f_invs = at("f_invs", [128, 512], F32, 166 * K)
        self.f_rq = at("f_rq", [128, 2 * 512], F32, 168 * K)
        self.f_qq = at("f_qq", [128, 1 * 512], F32, 172 * K)
        self.f_mtok = at("f_mtok", [128, 16], F32, 174 * K)
        self.f_sp = at("f_sp", [128, 2 * 512], F16, 160 * K)
        self.f_ffT = at("f_ffT", [128, 512], F16, 162 * K)
        self.f_ffT2 = at("f_ffT2", [128, 512], F16, 163 * K)
        WQB = pe - AB - (2 * 1024 * 4 + 1024 * 4 + 1024 * 4 + 1024 * 2 + 1024 * 2)
        self.f_macc = at("f_macc", [128, 6 * 512], F32, WQB)
        self.f_abs = at("f_abs", [128, 2 * 512], F32, WQB + 12 * K)
        self.f_uq = at("f_uq", [128, 2 * 512], F32, WQB + 16 * K)

        self.PB = [Bank(nc.alloc_psum_tensor(f"psb{i}", [128, 512], F32).ap()) for i in range(6)]
        self.PT = Bank(nc.alloc_psum_tensor("ps_t", [128, 1024], F16).ap())
        self.PX = Bank(nc.alloc_psum_tensor("ps_x", [128, 512], F32).ap())

        names = ["sL", "sQ", "sG", "sCC", "sT", "sV", "sS", "sP", "sO",
                 "dCst", "dX8", "dHst", "dWT", "dHT0", "dHT1", "dWq0", "dWq1",
                 "dWst", "dZst0", "dZst1", "dWot", "dZsl0", "dZsl1", "dXr",
                 "dX1", "dTg0", "dTg1", "dDs0", "dDs1", "dFst", "dF2", "dFr",
                 "dAr"]
        self.sems = {}
        self._sem_ctx = []
        for n in names:
            cm = nc.semaphore(n)
            self.sems[n] = cm.__enter__()
            self._sem_ctx.append(cm)
        self.cnt = {n: 0 for n in names}
        self.mark = {}

    def build(self):
        with self.nc.Block() as block:
            block.sync(lambda e: self.emit("sync", e))
            block.tensor(lambda e: self.emit("tensor", e))
            block.vector(lambda e: self.emit("vector", e))
            block.scalar(lambda e: self.emit("scalar", e))
            block.gpsimd(lambda e: self.emit("gpsimd", e))
        return self.nc

    def emit(self, name, eng):
        self.cur = name
        self.e = eng
        self.cnt = {n: 0 for n in self.sems}
        self.mark = {}
        self._qreg = None
        for b in self.PB + [self.PT, self.PX]:
            b.free = []
        self.program()

    def program(self):
        self.ph0_consts()
        if self.stage >= 1 and self.sub not in (20,):
            self.ph1_ln1()
        if self.stage >= 2 and self.sub != 20:
            self.ph2_proj()
        if self.stage >= 5:
            self.ph5_wquant()
        if self.stage >= 3:
            self.ph3_attn()
        if self.stage >= 4:
            self.ph4_post()
        if self.stage >= 6:
            self.ph6_ffn()
        self.fin()

    def ph0_consts(self):
        self.dma("sync", "dCst", self.s_h128[:], self.c_h128.ap())
        self.dma("sync", "dCst", self.s_ident[:], self.c_ident.ap())
        self.dma("sync", "dCst", self.s_ident32[:], self.c_ident32.ap())
        self.dma("sync", "dCst", self.s_onesrow[:], self.c_onesrow.ap())
        self.dma("sync", "dCst", self.s_onescol[:], self.c_onescol.ap())
        self.dma("sync", "dCst", self.s_onescol_bf[:], self.c_onescol_bf.ap())
        self.dma("sync", "dCst", self.s_amask[:], self.c_amask.ap())
        self.dma("sync", "dCst", self.s_magic[:, :], self.c_magic.ap())
        self.mark["consts_L"] = self.cnt["dCst"]
        self.dma("gpsimd", "sG", self.wqkv_scr[0].rearrange("(h e) d -> h e d", h=HPC), self.wq)
        self.dma("gpsimd", "sG", self.wqkv_scr[1].rearrange("(h e) d -> h e d", h=HPC), self.wk)
        self.dma("gpsimd", "sG", self.wqkv_scr[2].rearrange("(h e) d -> h e d", h=HPC), self.wv)
        self.dma("gpsimd", "sG", self.wo_scr, self.wo)
        self.mark["wcast_G"] = self.cnt["sG"]

    def ph1_ln1(self):
        st = self.s_lnst
        for t in range(8):
            self.dma("sync", "dX8", self.l_x8[:, t * D:(t + 1) * D],
                     self.xb_own[t * 128:(t + 1) * 128, :])
        if self.sub in (22, 23):
            if self.sub == 23:
                return
            self.wt(["vector"], "dX8")
            for t in range(8):
                self.vop("tensor_copy", self.l_h[:, t * D:(t + 1) * D],
                         self.l_x8[:, t * D:(t + 1) * D])
            self.wt(["sync"], "sV")
            for t in range(8):
                self.dma("sync", "dHst", self.hag_in[t * 128:(t + 1) * 128, :],
                         self.l_h[:, t * D:(t + 1) * D])
            self.wt(["gpsimd"], "dHst")
            self.tinc("sCC", self.e.collective_compute(
                "AllGather", ALU.bypass, replica_groups=RG4,
                ins=[self.hag_in], outs=[self.hag_out]) if self.on("gpsimd") else None)
            self.mark["h_ready"] = self.cnt["sCC"]
            return
        self.wt(["vector"], "dX8")
        self.wt(["vector"], "dCst")
        for t in range(8):
            xt = self.l_x8[:, t * D:(t + 1) * D]
            self.vop("reduce_sum", st[:, t:t + 1], xt, AX.X)
            self.vop("scalar_tensor_tensor", self.c_u[:, :], xt, 1.0, xt,
                     ALU.mult, ALU.mult, accum_out=st[:, 8 + t:9 + t])
            self.dr("vector")
        self.dr("vector")
        if self.on("vector"):
            self.e.tensor_scalar(st[:, 16:24], st[:, 0:8], INV2048, None, ALU.mult)
            self.e.tensor_scalar(st[:, 24:32], st[:, 8:16], INV2048, None, ALU.mult)
            self.e.drain()
            self.e.tensor_tensor(st[:, 40:48], st[:, 16:24], st[:, 16:24], ALU.mult)
            self.e.drain()
            self.e.scalar_tensor_tensor(st[:, 32:40], st[:, 24:32], 1.0, st[:, 40:48],
                                        ALU.mult, ALU.subtract)
            self.e.drain()
            self.e.tensor_scalar(self.s_small[:, 0:8], st[:, 32:40], EPS_LN, None, ALU.add)
            self.e.drain()
            self.e.reciprocal(self.s_small[:, 8:16], self.s_small[:, 0:8])
            self.e.drain()
        self.vop("nop")
        self.wt(["scalar"], "sV")
        self.sop_d("activation", st[:, 40:48], self.s_small[:, 8:16], AF.Sqrt)
        self.wt(["vector"], "sS")
        for t in range(8):
            self.vop("tensor_scalar", self.l_h[:, t * D:(t + 1) * D],
                     self.l_x8[:, t * D:(t + 1) * D],
                     st[:, 16 + t:17 + t], st[:, 40 + t:41 + t], ALU.subtract, ALU.mult)
        self.wt(["sync"], "sV")
        for t in range(8):
            self.dma("sync", "sL", self.hag_in[t * 128:(t + 1) * 128, :],
                     self.l_h[:, t * D:(t + 1) * D])
        self.wt(["gpsimd"], "sL")
        self.tinc("sCC", self.e.collective_compute(
            "AllGather", ALU.bypass, replica_groups=RG4,
            ins=[self.hag_in], outs=[self.hag_out]) if self.on("gpsimd") else None)
        self.mark["h_ready"] = self.cnt["sCC"]
        if "h" in self.tap_t:
            self.wt(["sync"], "sCC")
            self.dma("sync", "sO", self.tap_t["h"], self.hag_out)

    def _wq_chunks(self):
        chunks = []
        for i in range(8):
            for j in range(2):
                chunks.append(("g", self.wg_sl[i * 128:(i + 1) * 128, j * 1024:(j + 1) * 1024],
                               self.tg_in[j * 1024:(j + 1) * 1024, i * 128:(i + 1) * 128]))
        for i in range(8):
            for j in range(2):
                chunks.append(("u", self.wu_sl[i * 128:(i + 1) * 128, j * 1024:(j + 1) * 1024],
                               self.tu_in[j * 1024:(j + 1) * 1024, i * 128:(i + 1) * 128]))
        for r in range(2):
            for j in range(8):
                chunks.append(("d", self.wd_sl[r * 128:(r + 1) * 128, j * 1024:(j + 1) * 1024],
                               self.td_in[j * 1024:(j + 1) * 1024, r * 128:(r + 1) * 128]))
        return chunks

    def ph5_wquant(self):
        chunks = self._wq_chunks()
        self.vop("memset", self.s_small[:, 16:64], 0.0)
        self.mark["wq_zero"] = self.cnt["sV"]
        self.wt(["scalar"], "sV", self.mark["wq_zero"])
        for idx, (w, src, dst) in enumerate(chunks):
            buf = idx % 2
            wc = self.w_chunk[:, buf * 1024:(buf + 1) * 1024]
            dump = self.w_zq if idx % 2 == 0 else self.w_r
            if idx >= 2:
                self.wt(["sync"], "sS", self.mark[f"wqa_{idx - 2}"])
            self.dma("sync", f"dWq{buf}", wc, src)
            self.wt(["scalar"], f"dWq{buf}")
            if idx >= 2:
                self.wt(["scalar"], "sV", self.mark[f"wqared_{idx - 2}"])
            self.sop_d("activation", dump[:, :], wc, AF.Abs)
            self.mark[f"wqa_{idx}"] = self.cnt["sS"]
            self.wt(["vector"], "sS", self.mark[f"wqa_{idx}"])
            self.vop("tensor_reduce", self.s_small[:, 16 + idx:17 + idx], dump[:, :],
                     AX.X, ALU.add)
            self.mark[f"wqared_{idx}"] = self.cnt["sV"]
        if "ssm" in self.tap_t:
            self.wt(["sync"], "sV")
            self.dma("sync", "sO", self.tap_t["ssm"], self.s_small[:, :])
        self.wt(["vector"], "sS")
        if self.on("vector"):
            self.e.memset(self.s_small[:, 11:12], 0.0)
            self.e.reduce_sum(self.s_small[:, 8:9], self.s_small[:, 16:32], AX.X)
            self.e.reduce_sum(self.s_small[:, 9:10], self.s_small[:, 32:48], AX.X)
            self.e.reduce_sum(self.s_small[:, 10:11], self.s_small[:, 48:64], AX.X)
            self.e.drain()
            self.e.tensor_copy(self.s_bhl[:, 0:4], self.s_small[:, 8:12])
            self.e.drain()
            self.e.tensor_tensor(self.s_bhl[:, 4:8], self.s_small[:, 8:12],
                                 self.s_bhl[:, 0:4], ALU.subtract)
            self.e.drain()
        self.vop("nop")
        self.wt(["tensor"], "sV")
        self.bank_acquire(self.PX)
        if self.on("tensor"):
            self.e.matmul(self.PX.ap[0:1, 0:4], self.s_onescol_bf[:, :], self.s_bhl[:, 0:4],
                          start=True, stop=False)
        self.tinc("sT", self.e.matmul(self.PX.ap[0:1, 0:4], self.s_onescol_bf[:, :],
                                      self.s_bhl[:, 4:8], start=False, stop=True)
                  if self.on("tensor") else None)
        self.wt(["vector"], "sT")
        self.vop("tensor_copy", self.s_row[0:1, 0:4], self.PX.ap[0:1, 0:4])
        self.bank_release(self.PX, "sV")
        self.wt(["sync"], "sV")
        self.dma("sync", "dAr", self.ar_in[:], self.s_row[0:1, 0:4])
        self.wt(["gpsimd"], "dAr")
        self.tinc("sCC", self.e.collective_compute(
            "AllReduce", ALU.add, replica_groups=RG8,
            ins=[self.ar_in], outs=[self.ar_out]) if self.on("gpsimd") else None)
        self.wt(["sync"], "sCC")
        if "arr" in self.tap_t:
            self.dma("sync", "sO", self.tap_t["arr"], self.ar_out[:])
        self.dma("sync", "dAr", self.s_row[0:1, 8:12], self.ar_out[:])
        self.wt(["vector"], "dAr")
        self.bcast_row(self.s_row[0:1, 8:12], self.s_rhi[0:1, 0:4], self.s_rlo[0:1, 0:4],
                       self.PX, 4, exact=True)
        self.wt(["vector"], "sT")
        if self.on("vector"):
            self.e.tensor_scalar(self.s_small[:, 44:48], self.PX.ap[0:128, 0:4],
                                 INV2_24, None, ALU.mult)
            self.e.drain()
            self.e.tensor_scalar(self.s_wsc[:, 0:4], self.s_small[:, 44:48], EPS_Q, None, ALU.max)
            self.e.drain()
            self.e.reciprocal(self.s_wsc[:, 4:8], self.s_wsc[:, 0:4])
            self.e.drain()
        self.vop("nop")
        self.bank_release(self.PX, "sV")
        self.mark["wsc_ready"] = self.cnt["sV"]
        if "wsc" in self.tap_t:
            self.wt(["sync"], "sV")
            self.dma("sync", "sO", self.tap_t["wsc"], self.s_wsc[0:1, :])

        widx_of = {"g": 0, "u": 1, "d": 2}
        last_store = 0
        for idx, (w, src, dst) in enumerate(chunks):
            buf = idx % 2
            wc = self.w_chunk[:, buf * 1024:(buf + 1) * 1024]
            if idx >= 2:
                self.wt(["sync"], "sS", self.mark[f"wqq_{idx - 2}"])
            else:
                self.wt(["sync"], "sV", self.mark[f"wqared_{46 + idx}"])
            self.dma("sync", f"dWq{buf}", wc, src)
            self.wt(["scalar"], f"dWq{buf}")
            self.wt(["scalar"], "sV", self.mark["wsc_ready"])
            if idx >= 1:
                self.wt(["scalar"], "sV", self.mark[f"wqr_{idx - 1}"])
            wi = widx_of[w]
            self.sop_d("activation", self.w_zq[:, :], wc, AF.Identity,
                       bias=self.s_magic[:, 0:1], scale=self.s_wsc[:, 4 + wi:5 + wi])
            self.mark[f"wqq_{idx}"] = self.cnt["sS"]
            self.wt(["vector"], "sS")
            self.vop("tensor_scalar", self.w_r[:, :], self.w_zq[:, :], MAGIC, -1.0,
                     ALU.subtract, ALU.max)
            self.mark[f"wqr_{idx}"] = self.cnt["sV"]
            self.dr("vector")
            if idx >= 1:
                self.wt(["vector"], "sT", self.mark[f"wqt_{idx - 1}"])
            self.vop("tensor_scalar", self.w_t[:, :], self.w_r[:, :], 1.0, None, ALU.min)
            self.mark[f"wqm_{idx}"] = self.cnt["sV"]
            self.wt(["tensor"], "sV", self.mark[f"wqm_{idx}"])
            self.bank_acquire(self.PT)
            for c8 in range(8):
                self.tinc("sT", self.e.transpose(
                    self.PT.ap[:, c8 * 128:(c8 + 1) * 128],
                    self.w_t[:, c8 * 128:(c8 + 1) * 128], self.s_ident[:, :])
                    if self.on("tensor") else None)
            self.mark[f"wqt_{idx}"] = self.cnt["sT"]
            self.wt(["vector"], "sT")
            self.wt(["vector"], "dWst", last_store)
            self.vop("tensor_copy", self.w_stage[:, :], self.PT.ap[:, :])
            self.bank_release(self.PT, "sV")
            self.wt(["sync"], "sV")
            self.dma("sync", "dWst", dst.rearrange("(c p) f -> p c f", p=128),
                     self.w_stage.rearrange("p (c f) -> p c f", c=8))
            last_store = self.cnt["dWst"]
        self.wt(["gpsimd"], "dWst")
        for i_, o_ in ((self.tg_in, self.tg_out), (self.tu_in, self.tu_out),
                       (self.td_in, self.td_out)):
            self.tinc("sCC", self.e.collective_compute(
                "AllGather", ALU.bypass, replica_groups=RG8,
                ins=[i_], outs=[o_]) if self.on("gpsimd") else None)
        self.mark["t_ready"] = self.cnt["sCC"]
        if "tg" in self.tap_t:
            self.wt(["sync"], "sCC")
            self.dma("sync", "sO", self.tap_t["tg"], self.tg_in)
        if "td" in self.tap_t:
            self.wt(["sync"], "sCC")
            self.dma("sync", "sO", self.tap_t["td"], self.td_in)

    def ph2_proj(self):
        self.wt(["sync"], "sG", self.mark["wcast_G"])
        self.wt(["sync"], "sCC", self.mark["h_ready"])
        for p3, dst in ((0, self.a_wqt), (1, self.a_wkt), (2, self.a_wvt)):
            for dc in range(16):
                self.dmat("sync", "dWT", dst[:, dc * 512:(dc + 1) * 512],
                          self.wqkv_scr[p3][0:512, dc * 128:(dc + 1) * 128])
        bank_i = 0
        for eh in range(8):
            buf = eh % 2
            hT = self.a_hT[:, buf * 8192:(buf + 1) * 8192]
            if eh >= 2:
                self.wt(["sync"], "sT", self.mark[f"proj_T_{eh - 2}"])
            for dc in range(16):
                self.dmat("sync", f"dHT{buf}", hT[:, dc * 512:(dc + 1) * 512],
                          self.hag_out[eh * 512:(eh + 1) * 512, dc * 128:(dc + 1) * 128])
            self.mark[f"hT_L_{eh}"] = self.cnt[f"dHT{buf}"]
            self.wt(["tensor"], f"dHT{buf}", self.mark[f"hT_L_{eh}"])
            if eh == 0:
                self.wt(["tensor"], "dWT")
            for wt_, dst in ((self.a_wqt, self.a_qt), (self.a_wkt, self.a_kt)):
                for h in range(HPC):
                    b = self.PB[bank_i % 6]
                    bank_i += 1
                    self.bank_acquire(b)
                    for dc in range(16):
                        self.tinc("sT", self.e.matmul(
                            b.ap[:, :], wt_[:, dc * 512 + h * 128: dc * 512 + (h + 1) * 128],
                            hT[:, dc * 512:(dc + 1) * 512],
                            start=(dc == 0), stop=(dc == 15)) if self.on("tensor") else None,
                            k=(1 if dc == 15 else 0))
                    self.wt(["vector"], "sT")
                    self.vop("tensor_copy",
                             dst[:, h * S + eh * 512: h * S + (eh + 1) * 512], b.ap[:, :])
                    self.bank_release(b, "sV")
            for kt4 in range(4):
                b = self.PB[bank_i % 6]
                bank_i += 1
                self.bank_acquire(b)
                for dc in range(16):
                    self.tinc("sT", self.e.matmul(
                        b.ap[:, :], hT[:, dc * 512 + kt4 * 128: dc * 512 + (kt4 + 1) * 128],
                        self.a_wvt[:, dc * 512:(dc + 1) * 512],
                        start=(dc == 0), stop=(dc == 15)) if self.on("tensor") else None,
                        k=(1 if dc == 15 else 0))
                self.wt(["vector"], "sT")
                self.vop("tensor_copy",
                         self.a_v[:, (eh * 4 + kt4) * 512:(eh * 4 + kt4 + 1) * 512], b.ap[:, :])
                self.bank_release(b, "sV")
            self.mark[f"proj_T_{eh}"] = self.cnt["sT"]
        self.mark["proj_V"] = self.cnt["sV"]
        if "hT" in self.tap_t:
            self.wt(["sync"], "sT", self.mark["proj_T_7"])
            self.dma("sync", "sO", self.tap_t["hT"], self.a_hT[:, 0:8192])
        if "wqt" in self.tap_t:
            self.wt(["sync"], "sT", self.mark["proj_T_7"])
            self.dma("sync", "sO", self.tap_t["wqt"], self.a_wqt[:, :])
        self._tap_proj()

    def _tap_proj(self):
        if "qt" in self.tap_t:
            self.wt(["sync"], "sV", self.mark["proj_V"])
            for h in range(HPC):
                self.dma("sync", "sO", self.tap_t["qt"][h * E:(h + 1) * E, :],
                         self.a_qt[:, h * S:(h + 1) * S])
                self.dma("sync", "sO", self.tap_t["kt"][h * E:(h + 1) * E, :],
                         self.a_kt[:, h * S:(h + 1) * S])
        if "v" in self.tap_t:
            self.wt(["sync"], "sV", self.mark["proj_V"])
            for kt4 in range(32):
                self.dma("sync", "sO", self.tap_t["v"][kt4 * 128:(kt4 + 1) * 128, :],
                         self.a_v[:, kt4 * 512:(kt4 + 1) * 512])

    def ph3_attn(self):
        hrange = 1 if self.sub < 90 else HPC
        qtrange = 4 if self.sub < 90 else 32
        for h in range(hrange):
            for qt in range(qtrange):
                klen = (qt + 1) * 128
                nkb = (klen + 511) // 512
                qcol = h * S + qt * 128
                for kb in range(nkb):
                    w = min(512, klen - kb * 512)
                    bS = self.PB[kb % 2]
                    self.bank_acquire(bS)
                    self.tinc("sT", self.e.matmul(
                        bS.ap[:, 0:w], self.a_qt[:, qcol:qcol + 128],
                        self.a_kt[:, h * S + kb * 512: h * S + kb * 512 + w],
                        start=True, stop=True) if self.on("tensor") else None)
                    last = (kb == nkb - 1)
                    if self.sub == 6:
                        self.wt(["vector"], "sT")
                        self.vop("tensor_copy", self.a_sS[:, kb * 512: kb * 512 + w], bS.ap[:, 0:w])
                        self.bank_release(bS, "sV")
                        continue
                    if w > 128:
                        self.wt(["scalar"], "sT")
                        self.sop_d("copy", self.a_sS[:, kb * 512: kb * 512 + (w - 128 if last else w)],
                                   bS.ap[:, 0:(w - 128 if last else w)])
                        self.mark["sc_evac"] = self.cnt["sS"]
                    if self.sub == 7:
                        if w > 128:
                            self.bank_release(bS, "sS")
                        else:
                            self.wt(["vector"], "sT")
                            self.vop("tensor_copy", self.a_sS[:, kb * 512: kb * 512 + w], bS.ap[:, 0:w])
                            self.bank_release(bS, "sV")
                        continue
                    if last:
                        self.wt(["vector"], "sT")
                        if w > 128:
                            self.wt(["vector"], "sS", self.mark["sc_evac"])
                        self.vop("scalar_tensor_tensor", self.a_sS[:, klen - 128:klen],
                                 bS.ap[:, w - 128:w], 1.0, self.s_amask[:, :],
                                 ALU.mult, ALU.add)
                        self.bank_release(bS, "sV")
                        if w > 128:
                            self.bank_release(bS, "sS", add=True)
                    else:
                        self.bank_release(bS, "sS")
                    if self.sub == 8:
                        continue
                    if w > 128:
                        self.wt(["vector"], "sS")
                    self.vop("tensor_reduce", self.a_mx[:, kb:kb + 1],
                             self.a_sS[:, kb * 512: kb * 512 + w], AX.X, ALU.max)
                if self.sub in (2, 6, 7, 8):
                    continue
                self.dr("vector")
                if self.on("vector"):
                    self.e.tensor_reduce(self.a_mx[:, 8:9], self.a_mx[:, 0:nkb], AX.X, ALU.max)
                    self.e.drain()
                    self.e.tensor_scalar(self.a_mx[:, 9:10], self.a_mx[:, 8:9], -C_SM, None, ALU.mult)
                    self.e.drain()
                self.vop("nop")
                self.mark["sm_m"] = self.cnt["sV"]
                self.wt(["scalar"], "sV", self.mark["sm_m"])
                if "pfree" in self.mark:
                    self.wt(["scalar"], "sV", self.mark["pfree"])
                self.sop_d("activation", self.a_P[:, 0:klen], self.a_sS[:, 0:klen], AF.Exp,
                           bias=self.a_mx[:, 9:10], scale=C_SM, accum_out=self.a_mx[:, 10:11])
                self.mark["sm_e"] = self.cnt["sS"]
                self.wt(["vector"], "sS", self.mark["sm_e"])
                if self.on("vector"):
                    self.e.reciprocal(self.a_mx[:, 11:12], self.a_mx[:, 10:11])
                    self.e.drain()
                self.vop("nop")
                if "ptT_T" in self.mark:
                    self.wt(["vector"], "sT", self.mark["ptT_T"])
                self.vop("tensor_scalar", self.a_Pn[:, 0:klen], self.a_P[:, 0:klen],
                         self.a_mx[:, 11:12], None, ALU.mult)
                self.mark["pfree"] = self.cnt["sV"]
                self.mark["pn_v"] = self.cnt["sV"]
                if self.sub == 3:
                    continue
                nbat = (qt + 1 + 7) // 8
                for bt_ in range(nbat):
                    n8 = min(8, qt + 1 - bt_ * 8)
                    self.wt(["tensor"], "sV", self.mark["pn_v"])
                    self.bank_acquire(self.PT)
                    for j in range(n8):
                        kc = bt_ * 8 + j
                        self.tinc("sT", self.e.transpose(
                            self.PT.ap[:, j * 128:(j + 1) * 128],
                            self.a_Pn[:, kc * 128:(kc + 1) * 128], self.s_ident[:, :])
                            if self.on("tensor") else None)
                    self.wt(["vector"], "sT")
                    self.vop("tensor_copy", self.a_ptT[:, bt_ * 1024: bt_ * 1024 + n8 * 128],
                             self.PT.ap[:, 0:n8 * 128])
                    self.bank_release(self.PT, "sV")
                self.mark["ptT_v"] = self.cnt["sV"]
                if self.sub == 4:
                    continue
                bz = self.PB[4 + qt % 2]
                self.bank_acquire(bz)
                self.wt(["tensor"], "sV", self.mark["ptT_v"])
                for kc in range(qt + 1):
                    self.tinc("sT", self.e.matmul(
                        bz.ap[:, 0:128],
                        self.a_v[:, kc * 512 + h * 128: kc * 512 + (h + 1) * 128],
                        self.a_ptT[:, kc * 128:(kc + 1) * 128],
                        start=(kc == 0), stop=(kc == qt)) if self.on("tensor") else None,
                        k=(1 if kc == qt else 0))
                self.mark["ptT_T"] = self.cnt["sT"]
                self.wt(["vector"], "sT")
                strip = self.a_zb[:, (qt // 4 % 2) * 512:((qt // 4 % 2) + 1) * 512]
                if f"zst_{qt // 4 % 2}" in self.mark:
                    self.wt(["vector"], f"dZst{qt // 4 % 2}", self.mark[f"zst_{qt // 4 % 2}"])
                self.vop("tensor_copy", strip[:, (qt % 4) * 128:(qt % 4 + 1) * 128],
                         bz.ap[:, 0:128])
                self.bank_release(bz, "sV")
                if qt % 4 == 3:
                    qb = qt // 4
                    self.wt(["sync"], "sV")
                    self.dma("sync", f"dZst{qb % 2}",
                             self.zag_in[h * E:(h + 1) * E, qb * 512:(qb + 1) * 512], strip)
                    self.mark[f"zst_{qb % 2}"] = self.cnt[f"dZst{qb % 2}"]
        self.mark["z_V"] = self.cnt["sV"]
        self.wt(["gpsimd"], "dZst0")
        self.wt(["gpsimd"], "dZst1")
        self.tinc("sCC", self.e.collective_compute(
            "AllGather", ALU.bypass, replica_groups=RG4,
            ins=[self.zag_in], outs=[self.zag_out]) if self.on("gpsimd") else None)
        self.mark["z_ready"] = self.cnt["sCC"]
        if "z" in self.tap_t:
            self.wt(["sync"], "sCC", self.mark["z_ready"])
            self.dma("sync", "sO", self.tap_t["z"], self.zag_in)

    def _own_q(self):
        if self._qreg is None:
            pid = self.e.partition_id()
            self._qreg = self.e.snap(pid % 4, min_val=0, max_val=3)
        return self._qreg

    def ph4_post(self):
        st = self.s_lnst
        self.wt(["sync"], "sCC", self.mark["z_ready"])
        for h in range(16):
            self.dmat("sync", "dWot", self.c_wot[:, h * D:(h + 1) * D], self.wo_scr[h])
        self.mark["wot_L"] = self.cnt["dWot"]

        for qt in range(8):
            zbuf = qt % 2
            zdst = self.c_zsl.rearrange("p (b h f) -> p b h f", b=2, h=16)[:, zbuf]
            if qt >= 2:
                self.wt(["sync"], "sT", self.mark[f"op_T_{qt - 2}"])
            if self.on("sync"):
                src = self.zag_out.rearrange("(h p) s -> p h s", p=128)[
                    :, :, bass.ds(self._own_q() * TPC + qt * 128, 128)]
            else:
                src = self.zag_out.rearrange("(h p) s -> p h s", p=128)[:, :, 0:128]
            self.dma("sync", f"dZsl{zbuf}", zdst, src)
            self.mark[f"zsl_L_{qt}"] = self.cnt[f"dZsl{zbuf}"]
            xrb = self.c_xr[:, :]
            if qt >= 1:
                self.wt(["scalar"], "sV", self.mark[f"x1_{qt - 1}"])
            self.wt(["sync"], "sV", self.mark[f"x1_{qt - 1}"] if qt >= 1 else self.mark["z_V"])
            self.dma("sync", "dXr", xrb, self.xb_own[qt * 128:(qt + 1) * 128, :])
            self.mark[f"xr_L_{qt}"] = self.cnt["dXr"]
            self.wt(["tensor"], "dWot", self.mark["wot_L"])
            self.wt(["tensor"], f"dZsl{zbuf}", self.mark[f"zsl_L_{qt}"])
            bs = [self.PB[j] for j in range(4)]
            for b in bs:
                self.bank_acquire(b)
            for j in range(4):
                for h in range(16):
                    self.tinc("sT", self.e.matmul(
                        bs[j].ap[:, :],
                        self.c_zsl[:, zbuf * 2048 + h * 128: zbuf * 2048 + (h + 1) * 128],
                        self.c_wot[:, h * D + j * 512: h * D + (j + 1) * 512],
                        start=(h == 0), stop=(h == 15)) if self.on("tensor") else None,
                        k=(1 if h == 15 else 0))
            self.mark[f"op_T_{qt}"] = self.cnt["sT"]
            self.wt(["vector"], "sT")
            for j in range(4):
                self.vop("tensor_copy", self.c_ao[:, j * 512:(j + 1) * 512], bs[j].ap[:, :])
                self.bank_release(bs[j], "sV")
            if "ao" in self.tap_t:
                self.wt(["sync"], "sV")
                self.dma("sync", "sO", self.tap_t["ao"][qt * 128:(qt + 1) * 128, :], self.c_ao[:, :])
            if "zsl" in self.tap_t:
                self.wt(["sync"], f"dZsl{zbuf}", self.mark[f"zsl_L_{qt}"])
                self.dma("sync", "sO", self.tap_t["zsl"][qt * 128:(qt + 1) * 128, :],
                         self.c_zsl[:, zbuf * 2048:(zbuf + 1) * 2048])
            self.dr("vector")
            self._ln_stats(self.c_ao)
            self.wt(["vector"], "dXr", self.mark[f"xr_L_{qt}"])
            if self.on("vector"):
                self.e.tensor_scalar(self.c_u[:, :], self.c_ao[:, :], st[:, 0:1], st[:, 1:2],
                                     ALU.subtract, ALU.mult)
                self.e.drain()
                self.e.tensor_tensor(self.c_x1[:, :], self.c_u[:, :], xrb, ALU.add)
                self.e.drain()
            self.vop("nop")
            self.mark[f"x1_{qt}"] = self.cnt["sV"]
            if "xr" in self.tap_t:
                self.wt(["sync"], "sV", self.mark[f"x1_{qt}"])
                self.dma("sync", "sO", self.tap_t["xr"][qt * 128:(qt + 1) * 128, :], xrb)
            self.wt(["sync"], "sV", self.mark[f"x1_{qt}"])
            self.dma("sync", "dX1", self.x1buf[qt * 128:(qt + 1) * 128, :], self.c_x1[:, :])
            if "x1" in self.tap_t:
                self.dma("sync", "sO", self.tap_t["x1"][qt * 128:(qt + 1) * 128, :],
                         self.c_x1[:, :])
            self._ln_stats(self.c_x1)
            if self.on("vector"):
                self.e.tensor_scalar(self.c_h2[:, :], self.c_x1[:, :], st[:, 0:1], st[:, 1:2],
                                     ALU.subtract, ALU.mult)
                self.e.drain()
                self.e.reduce_max(self.s_small[:, 0:1], self.c_h2[:, :], AX.X,
                                  apply_absolute_value=True)
                self.e.drain()
                self.e.tensor_scalar(self.s_small[:, 1:2], self.s_small[:, 0:1], EPS_Q,
                                     None, ALU.max)
                self.e.drain()
                self.e.reciprocal(self.s_small[:, 2:3], self.s_small[:, 1:2])
                self.e.tensor_scalar(self.s_small[:, 4:5], self.s_small[:, 1:2], INV128,
                                     None, ALU.mult)
                self.e.drain()
                self.e.tensor_scalar(self.s_small[:, 3:4], self.s_small[:, 2:3], 128.0,
                                     None, ALU.mult)
                self.e.drain()
                self.e.tensor_scalar(self.c_u[:, :], self.c_h2[:, :], self.s_small[:, 3:4],
                                     MAGIC, ALU.mult, ALU.add)
                self.e.drain()
                self.e.tensor_scalar(self.c_ao[:, :], self.c_u[:, :], MAGIC, -128.0,
                                     ALU.subtract, ALU.max)
                self.e.drain()
                self.e.tensor_scalar(self.c_xq[:, :], self.c_ao[:, :], 127.0,
                                     self.s_small[:, 4:5], ALU.min, ALU.mult)
                self.e.drain()
            self.vop("nop")
            self.mark[f"xq_{qt}"] = self.cnt["sV"]
            if "xq" in self.tap_t:
                self.wt(["sync"], "sV", self.mark[f"xq_{qt}"])
                self.dma("sync", "sO", self.tap_t["xq"][qt * 128:(qt + 1) * 128, :],
                         self.c_xq[:, :])
            for half in range(2):
                self.wt(["tensor"], "sV", self.mark[f"xq_{qt}"])
                self.bank_acquire(self.PT)
                for c8 in range(8):
                    dc = half * 8 + c8
                    self.tinc("sT", self.e.transpose(
                        self.PT.ap[:, c8 * 128:(c8 + 1) * 128],
                        self.c_xq[:, dc * 128:(dc + 1) * 128], self.s_ident[:, :])
                        if self.on("tensor") else None)
                self.wt(["vector"], "sT")
                outap = self.c_xqT.rearrange("p (dc t) -> p dc t", dc=16)[
                    :, half * 8:(half + 1) * 8, qt * 128:(qt + 1) * 128]
                self.vop("tensor_copy", outap, self.PT.ap[:, :])
                self.bank_release(self.PT, "sV")
        self.mark["xqT_V"] = self.cnt["sV"]
        self.mark["ph4_T"] = self.cnt["sT"]
        self.mark["x1st"] = self.cnt["dX1"]

    def _ln_stats(self, src_f32):
        st = self.s_lnst
        if self.on("vector"):
            self.e.reduce_sum(st[:, 2:3], src_f32[:, :], AX.X)
            self.e.drain()
            self.e.scalar_tensor_tensor(self.c_u[:, :], src_f32[:, :], 1.0, src_f32[:, :],
                                        ALU.mult, ALU.mult, accum_out=st[:, 3:4])
            self.e.drain()
            self.e.tensor_scalar(st[:, 0:1], st[:, 2:3], INV2048, None, ALU.mult)
            self.e.tensor_scalar(st[:, 4:5], st[:, 3:4], INV2048, None, ALU.mult)
            self.e.drain()
            self.e.tensor_tensor(st[:, 5:6], st[:, 0:1], st[:, 0:1], ALU.mult)
            self.e.drain()
            self.e.scalar_tensor_tensor(st[:, 6:7], st[:, 4:5], 1.0, st[:, 5:6],
                                        ALU.mult, ALU.subtract)
            self.e.drain()
            self.e.tensor_scalar(st[:, 7:8], st[:, 6:7], EPS_LN, None, ALU.add)
            self.e.drain()
            self.e.reciprocal(st[:, 8:9], st[:, 7:8])
            self.e.drain()
        self.vop("nop")
        self.wt(["scalar"], "sV")
        self.sop_d("activation", st[:, 1:2], st[:, 8:9], AF.Sqrt)
        self.wt(["vector"], "sS")

    def ph6_ffn(self):
        for th in range(2):
            self._ffn_gateup(th)
            self._ffn_fwht(th)
            self._ffn_actq2(th)
            self._ffn_down(th)

    def _ffn_gateup(self, th):
        self.wt(["scalar"], "sCC", self.mark["t_ready"])
        for g in range(32):
            buf = g % 2
            rank = (g * 256) // 1024
            col = (g * 256) % 1024
            for tout, tdst in ((self.tg_out, self.f_tg), (self.tu_out, self.f_tu)):
                if g >= 2:
                    self.wt(["sync"], "sT", self.mark[f"gu_T_{th}_{g - 2}"])
                elif th == 1:
                    self.wt(["sync"], "sT", self.mark["dn_T_0_15"])
                else:
                    self.wt(["sync"], "sT", self.mark["ph4_T"])
                    self.wt(["sync"], "dX1", self.mark["x1st"])
                src = tout[rank * D:(rank + 1) * D, col:col + 256] \
                    .rearrange("(dc p) f -> p dc f", p=128)
                self.dma("sync", f"dTg{buf}",
                         tdst[:, buf * 4096:(buf + 1) * 4096].rearrange("p (dc f) -> p dc f", dc=16),
                         src)
            self.mark[f"gu_L_{th}_{g}"] = self.cnt[f"dTg{buf}"]
            bg = [self.PB[0], self.PB[1]] if g % 2 == 0 else [self.PB[2], self.PB[3]]
            bu = [self.PB[4], self.PB[5]] if g % 2 == 0 else [self.PB[2], self.PB[3]]
            self.wt(["tensor"], f"dTg{buf}", self.mark[f"gu_L_{th}_{g}"])
            if th == 0 and g == 0:
                self.wt(["tensor"], "sV", self.mark["xqT_V"])
            for ic in range(2):
                self.bank_acquire(bg[ic])
                for dc in range(16):
                    self.tinc("sT", self.e.matmul(
                        bg[ic].ap[:, :],
                        self.f_tg[:, buf * 4096 + dc * 256 + ic * 128:
                                  buf * 4096 + dc * 256 + (ic + 1) * 128],
                        self.c_xqT[:, dc * TPC + th * 512: dc * TPC + (th + 1) * 512],
                        start=(dc == 0), stop=(dc == 15)) if self.on("tensor") else None,
                        k=(1 if dc == 15 else 0))
            self.wt(["scalar"], "sT")
            if g >= 1:
                self.wt(["scalar"], "sV", self.mark[f"gu_V_{th}_{g - 1}"])
            for ic in range(2):
                self.sop_d("activation", self.f_sg[:, ic * 512:(ic + 1) * 512],
                           bg[ic].ap[:, :], AF.Silu, scale=self.s_wsc[:, 0:1])
                self.bank_release(bg[ic], "sS")
            for ic in range(2):
                self.bank_acquire(bu[ic])
                for dc in range(16):
                    self.tinc("sT", self.e.matmul(
                        bu[ic].ap[:, :],
                        self.f_tu[:, buf * 4096 + dc * 256 + ic * 128:
                                  buf * 4096 + dc * 256 + (ic + 1) * 128],
                        self.c_xqT[:, dc * TPC + th * 512: dc * TPC + (th + 1) * 512],
                        start=(dc == 0), stop=(dc == 15)) if self.on("tensor") else None,
                        k=(1 if dc == 15 else 0))
            self.wt(["vector"], "sT")
            self.wt(["vector"], "sS")
            for ic in range(2):
                self.vop("scalar_tensor_tensor",
                         self.f_inter[:, ic * 512:(ic + 1) * 512], bu[ic].ap[:, :],
                         self.s_wsc[:, 1:2], self.f_sg[:, ic * 512:(ic + 1) * 512],
                         ALU.mult, ALU.mult)
                self.bank_release(bu[ic], "sV")
            self.mark[f"gu_V_{th}_{g}"] = self.cnt["sV"]
            if "inter" in self.tap_t and th == 0:
                self.wt(["sync"], "sV")
                for ic in range(2):
                    c = g * 2 + ic
                    self.dma("sync", "sO", self.tap_t["inter"][c * 128:(c + 1) * 128, :],
                             self.f_inter[:, ic * 512:(ic + 1) * 512])
            self.wt(["tensor"], "sV")
            for ic in range(2):
                c = g * 2 + ic
                bb = bg[ic]
                self.bank_acquire(bb)
                self.tinc("sT", self.e.matmul(
                    bb.ap[:, :], self.s_h128[:, :], self.f_inter[:, ic * 512:(ic + 1) * 512],
                    start=True, stop=True) if self.on("tensor") else None)
                self.wt(["scalar"], "sT")
                self.sop_d("copy", self.f_z[:, c * 512:(c + 1) * 512], bb.ap[:, :])
                self.bank_release(bb, "sS")
            self.mark[f"gu_T_{th}_{g}"] = self.cnt["sT"]
        self.mark[f"z1_S_{th}"] = self.cnt["sS"]

    def _zap(self, sl):
        if sl < 64:
            return self.f_z[:, sl * 512:(sl + 1) * 512]
        return self.f_sp[:, (sl - 64) * 512:(sl - 63) * 512]

    def _ffn_fwht(self, th):
        self.wt(["vector"], "sS", self.mark[f"z1_S_{th}"])
        slot = list(range(64))
        spare = [64, 65]
        for stg in range(6):
            hstep = 1 << stg
            for c in range(64):
                c2 = c ^ hstep
                if c2 < c:
                    continue
                sa, sb = slot[c], slot[c2]
                da, db = spare
                if self.on("vector"):
                    self.e.tensor_tensor(self._zap(da), self._zap(sa), self._zap(sb), ALU.add)
                    self.e.tensor_tensor(self._zap(db), self._zap(sa), self._zap(sb), ALU.subtract)
                spare = [sa, sb]
                slot[c], slot[c2] = da, db
            self.dr("vector")
        self.vop("nop")
        self.mark[f"fwht_V_{th}"] = self.cnt["sV"]
        self.fw_slot = list(slot)
        if "zz" in self.tap_t and th == 0:
            self.wt(["sync"], "sV", self.mark[f"fwht_V_{th}"])
            for c in range(64):
                self.dma("sync", "sO", self.tap_t["zz"][c * 128:(c + 1) * 128, :],
                         self._zap(slot[c]))

    def _ffn_actq2(self, th):
        slot = self.fw_slot
        self.wt(["scalar"], "sV", self.mark[f"fwht_V_{th}"])
        self.wt(["scalar"], "sO")
        for k in range(3):
            self.sop_d("activation", self.f_macc[:, k * 512:(k + 1) * 512],
                       self._zap(slot[k]), AF.Abs)
        self.mark[f"aq2_init_{th}"] = self.cnt["sS"]
        self.wt(["vector"], "sS", self.mark[f"aq2_init_{th}"])
        par_of = [0, 0, 0]
        for c in range(3, 64):
            ab = self.f_abs[:, (c % 2) * 512:((c % 2) + 1) * 512]
            if c >= 5:
                self.wt(["scalar"], "sV", self.mark[f"aq2v_{th}_{c - 2}"])
            self.sop_d("activation", ab, self._zap(slot[c]), AF.Abs)
            self.wt(["vector"], "sS")
            k = c % 3
            p = par_of[k]
            src_acc = self.f_macc[:, (k + 3 * p) * 512:(k + 3 * p + 1) * 512]
            dst_acc = self.f_macc[:, (k + 3 * (1 - p)) * 512:(k + 3 * (1 - p) + 1) * 512]
            self.vop("tensor_max", dst_acc, src_acc, ab)
            par_of[k] = 1 - p
            self.mark[f"aq2v_{th}_{c}"] = self.cnt["sV"]
        self.dr("vector")
        accs = [self.f_macc[:, (k + 3 * par_of[k]) * 512:(k + 3 * par_of[k] + 1) * 512]
                for k in range(3)]
        free0 = self.f_macc[:, (0 + 3 * (1 - par_of[0])) * 512:(0 + 3 * (1 - par_of[0]) + 1) * 512]
        self.vop("tensor_max", free0, accs[0], accs[1])
        self.dr("vector")
        macc = self.f_macc[:, (1 + 3 * (1 - par_of[1])) * 512:(1 + 3 * (1 - par_of[1]) + 1) * 512]
        self.vop("tensor_max", macc, free0, accs[2])
        self.dr("vector")
        self.mark[f"macc_{th}"] = self.cnt["sV"]
        bnk = self.PB[0]
        self.wt(["tensor"], "sV", self.mark[f"macc_{th}"])
        self.bank_acquire(bnk)
        for t4 in range(4):
            self.tinc("sT", self.e.transpose(
                bnk.ap[:, t4 * 128:(t4 + 1) * 128], macc[:, t4 * 128:(t4 + 1) * 128],
                self.s_ident32[:, :]) if self.on("tensor") else None)
        self.wt(["vector"], "sT")
        for t4 in range(4):
            self.vop("reduce_max", self.f_mtok[:, th * 8 + t4: th * 8 + t4 + 1],
                     bnk.ap[:, t4 * 128:(t4 + 1) * 128], AX.X)
        self.bank_release(bnk, "sV")
        self.dr("vector")
        mt = self.f_mtok
        b0 = th * 8
        if self.on("vector"):
            self.e.tensor_scalar(self.s_small[:, 16:20],
                                 mt[:, b0:b0 + 4], C_FW, None, ALU.mult)
            self.e.drain()
            self.e.tensor_scalar(self.s_small[:, 20:24], self.s_small[:, 16:20], EPS_Q,
                                 None, ALU.max)
            self.e.drain()
            self.e.reciprocal(self.s_small[:, 24:28], self.s_small[:, 20:24])
            self.e.tensor_scalar(self.s_small[:, 32:36], self.s_small[:, 20:24], INV128,
                                 None, ALU.mult)
            self.e.drain()
            self.e.tensor_scalar(self.s_small[:, 28:32], self.s_small[:, 24:28], SQRT2,
                                 None, ALU.mult)
            self.e.drain()
        self.vop("nop")
        self.mark[f"aqs_{th}"] = self.cnt["sV"]
        self.wt(["tensor"], "sV", self.mark[f"aqs_{th}"])
        b2 = self.PB[1]
        self.bank_acquire(b2)
        self.tinc("sT", self.e.transpose(b2.ap[0:4, 0:128], self.s_small[:, 28:32],
                                         self.s_ident32[:, :]) if self.on("tensor") else None)
        self.tinc("sT", self.e.transpose(b2.ap[0:4, 128:256], self.s_small[:, 32:36],
                                         self.s_ident32[:, :]) if self.on("tensor") else None)
        self.wt(["vector"], "sT")
        self.vop("tensor_copy", self.s_r4[0:4, 0:128], b2.ap[0:4, 0:128])
        self.vop("tensor_copy", self.s_r4[0:4, 128:256], b2.ap[0:4, 128:256])
        self.bank_release(b2, "sV")
        self.wt(["sync"], "sV")
        for seg in range(4):
            self.dma("sync", "sL", self.s_row[0:1, 8 + seg * 128: 8 + (seg + 1) * 128],
                     self.s_r4[seg:seg + 1, 0:128])
        self.mark[f"scs_row_{th}"] = self.cnt["sL"]
        self.wt(["vector"], "sL", self.mark[f"scs_row_{th}"])
        self.bcast_row(self.s_row[0:1, 8:520], self.s_rhi[0:1, 8:520], self.s_rlo[0:1, 8:520],
                       self.PX, 512, exact=True)
        self.wt(["vector"], "sT")
        self.vop("tensor_copy", self.f_scs[:, :], self.PX.ap[:, :])
        self.bank_release(self.PX, "sV")
        self.wt(["sync"], "sV")
        for seg in range(4):
            self.dma("sync", "sL", self.s_row[0:1, 8 + seg * 128: 8 + (seg + 1) * 128],
                     self.s_r4[seg:seg + 1, 128:256])
        self.mark[f"invs_row_{th}"] = self.cnt["sL"]
        self.wt(["vector"], "sL", self.mark[f"invs_row_{th}"])
        self.bcast_row(self.s_row[0:1, 8:520], self.s_rhi[0:1, 8:520], self.s_rlo[0:1, 8:520],
                       self.PX, 512, exact=True)
        self.wt(["vector"], "sT")
        self.vop("tensor_copy", self.f_invs[:, :], self.PX.ap[:, :])
        self.bank_release(self.PX, "sV")
        self.dr("vector")
        self.mark[f"aqb_{th}"] = self.cnt["sV"]

        self.wt(["vector"], "sV", self.mark[f"aqb_{th}"])
        self.wt(["scalar"], "sV", self.mark[f"aqb_{th}"])
        NC = 64
        for step in range(NC + 2):
            if step < NC:
                c = step
                self.vop("tensor_tensor", self.f_uq[:, (c % 2) * 512:((c % 2) + 1) * 512],
                         self._zap(slot[c]), self.f_scs[:, :], ALU.mult)
                self.mark[f"qA_{th}_{c}"] = self.cnt["sV"]
            if 1 <= step <= NC:
                c = step - 1
                self.wt(["scalar"], "sV", self.mark[f"qA_{th}_{c}"])
                self.sop("activation", self.f_abs[:, (c % 2) * 512:((c % 2) + 1) * 512],
                         self.f_uq[:, (c % 2) * 512:((c % 2) + 1) * 512], AF.Identity,
                         bias=self.s_magic[:, 0:1])
                self.dr("scalar")
                self.sop_d("activation", self.f_rq[:, (c % 2) * 512:((c % 2) + 1) * 512],
                           self.f_abs[:, (c % 2) * 512:((c % 2) + 1) * 512], AF.Identity,
                           bias=self.s_magic[:, 1:2])
                self.mark[f"qB_{th}_{c}"] = self.cnt["sS"]
            if step >= 2:
                c = step - 2
                self.wt(["vector"], "sS", self.mark[f"qB_{th}_{c}"])
                self.vop("tensor_scalar", self.f_qq[:, 0:512],
                         self.f_rq[:, (c % 2) * 512:((c % 2) + 1) * 512], -128.0, 127.0,
                         ALU.max, ALU.min)
                self.dr("vector")
                self.vop("tensor_tensor", self._zap(slot[c]),
                         self.f_qq[:, 0:512],
                         self.f_invs[:, :], ALU.mult)
                self.mark[f"qD_{th}_{c}"] = self.cnt["sV"]
        self.mark[f"xq2_{th}"] = self.cnt["sV"]
        if "xq2" in self.tap_t and th == 0:
            self.wt(["sync"], "sV", self.mark[f"xq2_{th}"])
            for c in range(64):
                self.dma("sync", "sO", self.tap_t["xq2"][c * 128:(c + 1) * 128, :],
                         self._zap(slot[c]))

    def _ffn_down(self, th):
        slot = self.fw_slot
        st = self.s_lnst
        for dc in range(16):
            buf = dc % 2
            if dc >= 2:
                self.wt(["sync"], "sT", self.mark[f"dn_T_{th}_{dc - 2}"])
            else:
                self.wt(["sync"], "sT", self.mark[f"gu_T_{th}_31"])
            rank = dc // 2
            src = self.td_out[rank * II: (rank + 1) * II, (dc % 2) * 128:(dc % 2 + 1) * 128] \
                .rearrange("(ic p) f -> p ic f", p=128)
            self.dma("sync", f"dDs{buf}",
                     self.f_dstream[:, buf * 8192:(buf + 1) * 8192].rearrange("p (ic f) -> p ic f", ic=64),
                     src)
            self.mark[f"dn_L_{th}_{dc}"] = self.cnt[f"dDs{buf}"]
            b = self.PB[2 + dc % 2]
            self.wt(["tensor"], f"dDs{buf}", self.mark[f"dn_L_{th}_{dc}"])
            self.wt(["tensor"], "sV", self.mark[f"xq2_{th}"])
            self.bank_acquire(b)
            for Ic in range(64):
                self.tinc("sT", self.e.matmul(
                    b.ap[:, :], self.f_dstream[:, buf * 8192 + Ic * 128: buf * 8192 + (Ic + 1) * 128],
                    self._zap(slot[Ic]), start=(Ic == 0), stop=(Ic == 63))
                    if self.on("tensor") else None, k=(1 if Ic == 63 else 0))
            self.mark[f"dn_T_{th}_{dc}"] = self.cnt["sT"]
            self.wt(["scalar"], "sT", self.mark[f"dn_T_{th}_{dc}"])
            self.sop_d("activation", self.f_ffT[:, :], b.ap[:, :], AF.Identity,
                       scale=self.s_wsc[:, 2:3])
            self.bank_release(b, "sS")
            self.mark[f"dn_S_{th}_{dc}"] = self.cnt["sS"]
            self.wt(["tensor"], "sS", self.mark[f"dn_S_{th}_{dc}"])
            self.bank_acquire(self.PT)
            for t4 in range(4):
                self.tinc("sT", self.e.transpose(
                    self.PT.ap[:, t4 * 128:(t4 + 1) * 128],
                    self.f_ffT[:, t4 * 128:(t4 + 1) * 128], self.s_ident[:, :])
                    if self.on("tensor") else None)
            self.wt(["vector"], "sT")
            if "ffst" in self.mark:
                self.wt(["vector"], "dFst", self.mark["ffst"])
            self.vop("tensor_copy", self.f_ffT2[:, :], self.PT.ap[:, 0:512])
            self.bank_release(self.PT, "sV")
            self.wt(["sync"], "sV")
            dstap = self.ffnbuf[th * 512:(th + 1) * 512, dc * 128:(dc + 1) * 128] \
                .rearrange("(t4 p) f -> p t4 f", p=128)
            self.dma("sync", "dFst", dstap, self.f_ffT2.rearrange("p (t4 f) -> p t4 f", t4=4))
            self.mark["ffst"] = self.cnt["dFst"]
        self.mark[f"ffn_st_{th}"] = self.cnt["dFst"]
        if "ffn" in self.tap_t:
            self.wt(["sync"], "dFst", self.mark[f"ffn_st_{th}"])
            self.dma("sync", "sO", self.tap_t["ffn"][th * 512:(th + 1) * 512, :],
                     self.ffnbuf[th * 512:(th + 1) * 512, :])
        for t4 in range(4):
            qt = th * 4 + t4
            xrb = self.c_xr[:, :]
            self.wt(["sync"], "sV")
            self.dma("sync", "dXr", xrb, self.x1buf[qt * 128:(qt + 1) * 128, :])
            self.mark[f"x1r_L_{qt}"] = self.cnt["dXr"]
            self.wt(["gpsimd"], "dFst", self.mark[f"ffn_st_{th}"])
            if t4 >= 1:
                self.wt(["gpsimd"], "sV", self.mark[f"ln4_{th}_{t4 - 1}"])
            self.dma("gpsimd", "dFr", self.c_h2[:, :], self.ffnbuf[qt * 128:(qt + 1) * 128, :])
            self.wt(["vector"], "dFr")
            self._ln_stats(self.c_h2)
            self.wt(["vector"], "dXr", self.mark[f"x1r_L_{qt}"])
            if self.on("vector"):
                self.e.tensor_scalar(self.c_u[:, :], self.c_h2[:, :], st[:, 0:1], st[:, 1:2],
                                     ALU.subtract, ALU.mult)
                self.e.drain()
                self.e.tensor_tensor(self.c_ao[:, :], self.c_u[:, :], xrb, ALU.add)
                self.e.drain()
            self.vop("nop")
            self.mark[f"ln4_{th}_{t4}"] = self.cnt["sV"]
            self.wt(["sync"], "sV")
            self.dma("sync", "sO", self.out[qt * 128:(qt + 1) * 128, :], self.c_ao[:, :])

    def fin(self):
        self.wt(["gpsimd"], "sO")


def build(stage=6, taps=(), sub=99):
    nc = bass.Bass("TRN2", target_bir_lowering=False)
    kb = KB(nc, stage=stage, taps=taps, sub=sub)
    kb.build()
    return nc, kb


_B, _S, _D = 2, 4096, 2048


def kernel(L0_attn_x, L0_attn_W_Q, L0_attn_W_K, L0_attn_W_V, L0_attn_W_O,
           L0_ffn_w_gate, L0_ffn_w_up, L0_ffn_w_down):
    from concourse.bass_utils import run_bass_kernel_spmd
    x = np.ascontiguousarray(np.asarray(L0_attn_x, dtype=np.float32))
    wqf = np.asarray(L0_attn_W_Q, dtype=np.float32)
    wkf = np.asarray(L0_attn_W_K, dtype=np.float32)
    wvf = np.asarray(L0_attn_W_V, dtype=np.float32)
    wof = np.ascontiguousarray(np.asarray(L0_attn_W_O, dtype=np.float32))
    wgf = np.asarray(L0_ffn_w_gate, dtype=np.float32)
    wuf = np.asarray(L0_ffn_w_up, dtype=np.float32)
    wdf = np.asarray(L0_ffn_w_down, dtype=np.float32)

    nc, kb = build(stage=6, taps=())
    in_maps = []
    for c in range(8):
        b, qq = c // 4, c % 4
        in_maps.append({
            "xb_own": np.ascontiguousarray(x[b, qq * TPC:(qq + 1) * TPC]),
            "wq": np.ascontiguousarray(wqf[4 * qq:4 * qq + 4]),
            "wk": np.ascontiguousarray(wkf[4 * qq:4 * qq + 4]),
            "wv": np.ascontiguousarray(wvf[4 * qq:4 * qq + 4]),
            "wo": wof,
            "wg_sl": np.ascontiguousarray(wgf[c * 1024:(c + 1) * 1024]),
            "wu_sl": np.ascontiguousarray(wuf[c * 1024:(c + 1) * 1024]),
            "wd_sl": np.ascontiguousarray(wdf[c * 256:(c + 1) * 256]),
        })
    res = run_bass_kernel_spmd(nc, in_maps, core_ids=list(range(8)))
    out = np.empty((_B, _S, _D), dtype=np.float32)
    for c in range(8):
        b, qq = c // 4, c % 4
        out[b, qq * TPC:(qq + 1) * TPC] = np.asarray(res.results[c]["out"])
    return out
